# version 21
# baseline (speedup 1.0000x reference)
import numpy as np
import concourse.bass as bass
import concourse.mybir as mybir
from concourse.bass_utils import run_bass_kernel_spmd

F32 = mybir.dt.float32
BF16 = mybir.dt.bfloat16
I32 = mybir.dt.int32
AF = mybir.ActivationFunctionType
ALU = mybir.AluOpType
AX = mybir.AxisListType

D = 4096
TCORE = 1024
NT = 4
NC = 32
EPS = 1e-5

def cA(p):
    return 3 if p == 0 else 2

def cV(p):
    return 5 if p == 0 else 6

def A_base(u, p, j):
    return u * 28 + sum(4 * cA(q) for q in range(p)) + j * cA(p)

def V_base(u, p, j):
    return u * 68 + sum(4 * cV(q) for q in range(p)) + j * cV(p)

def tr_base(u, p):
    return (u * 3 + p) * 128

def G_base(u, p):
    return (u * 3 + p) * 8

def E_base(u, p):
    return (u * 3 + p) * 32

import math as _math

def _mask_support_tiles():
    n, window, stripes = D, 6, 3
    m = np.zeros((n, n), dtype=bool)
    stride = int(_math.sqrt(n))
    for k in range(stripes):
        ps = stride * k
        for i in range(0, n, window):
            m[ps + i:ps + i + window, i:i + window] = True
    m |= m.T
    return [[bool(m[S * 128:(S + 1) * 128, tc * 512:(tc + 1) * 512].any())
             for tc in range(8)] for S in range(32)]

KEEP = _mask_support_tiles()
K_TILES = []
P_PAIRS = []
PAIR_TILES = []
for _sb in range(4):
    for _tc in range(8):
        _sts = [st for st in range(8) if KEEP[_sb * 8 + st][_tc]]
        if _sts:
            P_PAIRS.append((_sb, _tc))
            PAIR_TILES.append(list(range(len(K_TILES), len(K_TILES) + len(_sts))))
            for st in _sts:
                K_TILES.append((_sb, _tc, st, _sb * 8 + st))
NK = len(K_TILES)
NP = len(P_PAIRS)
K_BEFORE = [PAIR_TILES[qq][0] for qq in range(NP)] + [NK]
SB_FIRST = {}
for qq, (sb, tc) in enumerate(P_PAIRS):
    SB_FIRST.setdefault(sb, qq)
TC_FIRST = []
_seen = set()
for (sb, tc) in P_PAIRS:
    TC_FIRST.append(tc not in _seen)
    _seen.add(tc)
TC_STORE_IDX = {}
_n = 0
for qq, (sb, tc) in enumerate(P_PAIRS):
    if TC_FIRST[qq]:
        TC_STORE_IDX[tc] = _n
        _n += 1
STORES_THRU = []
_n = 0
for qq in range(NP):
    if TC_FIRST[qq]:
        _n += 1
    STORES_THRU.append(_n)
WT_DONE = []
_tot = [0, 0]
for qq in range(NP):
    _tot[qq % 2] += 512 if TC_FIRST[qq] else 16
    WT_DONE.append(_tot[qq % 2])
ATT_SC = [[S for S in range(32) if KEEP[S][oc]] for oc in range(8)]

def mv_slot_done(u, p, oc):
    g = (u * 3 + p) * 8 + oc
    return 16 * (g // 2 + 1)


def build_nc():
    nc = bass.Bass("TRN2", target_bir_lowering=False)

    x_in = nc.dram_tensor("x_in", [TCORE, D], F32, kind="ExternalInput").ap()
    w1 = nc.dram_tensor("w1", [D, D], F32, kind="ExternalInput").ap()
    w2T = nc.dram_tensor("w2T", [D, D], F32, kind="ExternalInput").ap()
    mask = nc.dram_tensor("mask", [D, D], F32, kind="ExternalInput").ap()
    b2 = nc.dram_tensor("b2", [1, D], F32, kind="ExternalInput").ap()
    qw = nc.dram_tensor("qw", [D, D], I32, kind="ExternalInput").ap()
    scal = nc.dram_tensor("scal", [D, 32], F32, kind="ExternalInput").ap()
    zer = nc.dram_tensor("zer", [D, 32], F32, kind="ExternalInput").ap()
    fb = nc.dram_tensor("fb", [1, D], F32, kind="ExternalInput").ap()
    ident_in = nc.dram_tensor("ident", [128, 128], F32, kind="ExternalInput").ap()
    y_out = nc.dram_tensor("y_out", [TCORE, D], F32, kind="ExternalOutput").ap()

    amat = nc.dram_tensor("amat", [D, D], BF16)
    wdqT = nc.dram_tensor("wdqT", [D, D], BF16)

    from contextlib import ExitStack
    es = ExitStack()

    def sem1(name):
        return es.enter_context(nc.semaphore(name))

    def semn(name, n):
        return [es.enter_context(nc.semaphore(f"{name}{i}")) for i in range(n)]

    ms = sem1("ms")
    q_dq = sem1("q_dq")
    p1_pe = sem1("p1_pe")
    p1_ev = sem1("p1_ev")
    p2_tr = sem1("p2_tr")
    p2_trv = sem1("p2_trv")
    p2_pe = sem1("p2_pe")
    p2_ev = sem1("p2_ev")
    p2_gl = sem1("p2_gl")
    ac_ln = sem1("ac_ln")
    ve_ln = sem1("ve_ln")
    ms_ld = sem1("ms_ld")
    id_ld = sem1("id_ld")
    wqtr = sem1("wqtr")
    wqev = sem1("wqev")
    q_ld = semn("q_ld", 2)
    q_st = semn("q_st", 2)
    w1_ld = sem1("w1_ld")
    wt_ld = semn("wt_ld", 2)
    mk_ld = semn("mk_ld", 2)
    p1_st = semn("p1_st", 2)
    x_ld = sem1("x_ld")
    mv_ld = semn("mv_ld", 2)
    st_out = sem1("st_out")

    with es:
        ident_bf = es.enter_context(nc.sbuf_tensor("ident_bf", [128, 128], BF16))
        ones_bf = es.enter_context(nc.sbuf_tensor("ones_bf", [1, 128], BF16))
        fb_bf = es.enter_context(nc.sbuf_tensor("fb_bf", [1, D], BF16))
        halves_bf = es.enter_context(nc.sbuf_tensor("halves_bf", [1, 128], BF16))
        b2_bf = es.enter_context(nc.sbuf_tensor("b2_bf", [1, D], BF16))
        stats = es.enter_context(nc.sbuf_tensor("stats", [128, 32], F32))
        sumcols = es.enter_context(nc.sbuf_tensor("sumcols", [128, 32], F32))
        with nc.Block() as block:
            from contextlib import ExitStack as _ES

            with _ES() as es_1:
                qt = es_1.enter_context(nc.sbuf_tensor("qt", [128, 2 * D], BF16))
                scl_sb = es_1.enter_context(nc.sbuf_tensor("scl_sb", [128, 1024], F32))
                zro_sb = es_1.enter_context(nc.sbuf_tensor("zro_sb", [128, 1024], F32))
                wout = es_1.enter_context(nc.sbuf_tensor("wout", [128, 2 * D], BF16))
                w1blk = es_1.enter_context(nc.sbuf_tensor("w1blk", [128, NC * 1024], BF16))
                w2t = es_1.enter_context(nc.sbuf_tensor("w2t", [128, 2 * NC * 512], BF16))
                mkt = es_1.enter_context(nc.sbuf_tensor("mkt", [128, 2 * 512], F32))
                asm2 = es_1.enter_context(nc.sbuf_tensor("asm2", [128, 2 * D], BF16))
                aev = es_1.enter_context(nc.sbuf_tensor("aev", [128, 2 * 512], BF16))
                pa = [es_1.enter_context(nc.psum_tensor(f"pa{i}", [128, 512], F32))
                      for i in range(2)]
                ptr = [es_1.enter_context(nc.psum_tensor(f"ptr{i}", [128, 128], BF16))
                       for i in range(4)]

                @block.gpsimd
                def _(g):
                    g.dma_start(out=ident_bf[:], in_=ident_in[:, :]).then_inc(id_ld, 16)
                    w1v = w1.rearrange("(c p) s -> p c s", p=128)
                    for h in range(4):
                        g.dma_start(
                            out=w1blk[:].rearrange("p (c s) -> p c s", s=1024)
                                [:, h * 8:(h + 1) * 8, 0:1024],
                            in_=w1v[:, h * 8:(h + 1) * 8, 0:1024],
                        ).then_inc(w1_ld, 16)
                    g.dma_start(out=b2_bf[:], in_=b2[:, :]).then_inc(ms_ld, 16)
                    g.dma_start(out=fb_bf[:], in_=fb[:, :]).then_inc(ms_ld, 16)
                    g.dma_start(out=scl_sb[:].rearrange("p (r g) -> p r g", g=32),
                                in_=scal.rearrange("(r p) g -> p r g", p=128)
                                ).then_inc(ms_ld, 16)
                    g.dma_start(out=zro_sb[:].rearrange("p (r g) -> p r g", g=32),
                                in_=zer.rearrange("(r p) g -> p r g", p=128)
                                ).then_inc(ms_ld, 16)
                    def qw_load(r):
                        if r == 0:
                            g.wait_ge(ms, 4)
                        if r >= 2:
                            g.wait_ge(q_dq, 32 * (r - 1))
                        g.dma_start(out=qt[:, (r % 2) * D:(r % 2) * D + D],
                                    in_=qw[r * 128:(r + 1) * 128, :]
                                    ).then_inc(q_ld[r % 2], 16)
                    done_sb = set()
                    for qq, (sb, tc) in enumerate(P_PAIRS):
                        if sb not in done_sb:
                            done_sb.add(sb)
                            if sb >= 1:
                                g.wait_ge(p1_pe, K_BEFORE[SB_FIRST[sb]])
                                for h in range(4):
                                    g.dma_start(
                                        out=w1blk[:]
                                            .rearrange("p (c s) -> p c s", s=1024)
                                            [:, h * 8:(h + 1) * 8, 0:1024],
                                        in_=w1v[:, h * 8:(h + 1) * 8,
                                                sb * 1024:(sb + 1) * 1024],
                                    ).then_inc(w1_ld, 16)
                        if qq >= 2:
                            g.wait_ge(p1_pe, K_BEFORE[qq - 1])
                        sl2 = (qq % 2) * NC * 512
                        g.dma_start(
                            out=w2t[:, sl2:sl2 + NC * 512]
                                .rearrange("p (c t) -> p c t", t=512),
                            in_=w2T.rearrange("(c p) t -> p c t", p=128)
                                [:, :, tc * 512:(tc + 1) * 512],
                        ).then_inc(wt_ld[qq % 2], 16)
                        for k in PAIR_TILES[qq]:
                            Sg = K_TILES[k][3]
                            g.wait_ge(p1_ev, k + 1)
                            g.dma_start(
                                out=amat[Sg * 128:(Sg + 1) * 128,
                                         tc * 512:(tc + 1) * 512],
                                in_=aev[:, (k % 2) * 512:(k % 2) * 512 + 512],
                            ).then_inc(p1_st[k % 2], 16)
                            if 4 <= k < 36:
                                qw_load(k - 4)

                @block.sync
                def _(sync):
                    for qq, (sb, tc) in enumerate(P_PAIRS):
                        for k in PAIR_TILES[qq]:
                            Sg = K_TILES[k][3]
                            if k >= 2:
                                sync.wait_ge(p1_ev, k - 1)
                            sync.dma_start(
                                out=mkt[:, (k % 2) * 512:(k % 2) * 512 + 512],
                                in_=mask[Sg * 128:(Sg + 1) * 128,
                                         tc * 512:(tc + 1) * 512],
                            ).then_inc(mk_ld[k % 2], 16)
                @block.vector
                def _(v):
                    v.memset(ones_bf[:], 1.0).then_inc(ms, 1)
                    v.memset(halves_bf[:], 0.5).then_inc(ms, 1)
                    v.wait_ge(ms_ld, 64)
                    v.tensor_scalar_mul(scl_sb[:], scl_sb[:], 0.5).then_inc(ms, 1)
                    v.wait_ge(ms, 3)
                    v.tensor_scalar_mul(zro_sb[:], zro_sb[:], 0.5).then_inc(ms, 1)
                    def dequant_tile(r):
                        v.wait_ge(q_ld[r % 2], 16 * (r // 2 + 1))
                        if r >= 2:
                            v.wait_ge(wqtr, 32 * (r - 1))
                        for gg in range(32):
                            sl = r % 2
                            v.tensor_scalar(
                                wout[:, sl * D + gg * 128: sl * D + gg * 128 + 128],
                                qt[:, sl * D + gg * 128: sl * D + gg * 128 + 128],
                                scl_sb[:, r * 32 + gg: r * 32 + gg + 1],
                                zro_sb[:, r * 32 + gg: r * 32 + gg + 1],
                                op0=ALU.mult, op1=ALU.add,
                            ).then_inc(q_dq, 1)
                    for k in range(NK):
                        v.wait_ge(p1_pe, k + 1)
                        v.wait_ge(mk_ld[k % 2], 16 * (k // 2 + 1))
                        if k >= 2:
                            v.wait_ge(p1_st[k % 2], 16 * (k // 2))
                        v.tensor_tensor(
                            aev[:, (k % 2) * 512:(k % 2) * 512 + 512],
                            pa[k % 2][:],
                            mkt[:, (k % 2) * 512:(k % 2) * 512 + 512],
                            op=ALU.mult,
                        ).then_inc(p1_ev, 1)
                        if 8 <= k < 40:
                            r = k - 8
                            if r >= 2:
                                v.wait_ge(q_st[r % 2], 16 * (r // 2))
                            for blk in range(NC):
                                b = r * NC + blk
                                v.wait_ge(wqtr, b + 1)
                                v.tensor_copy(
                                    asm2[:, (r % 2) * D + blk * 128:
                                         (r % 2) * D + blk * 128 + 128],
                                    ptr[b % 4][:],
                                ).then_inc(wqev, 1)
                        if 6 <= k < 38:
                            dequant_tile(k - 6)

                @block.scalar
                def _(s):
                    for r in range(32):
                        s.wait_ge(wqev, 32 * (r + 1))
                        s.dma_start(out=wdqT.rearrange("(c p) o -> p c o", p=128)
                                    [:, :, r * 128:(r + 1) * 128],
                                    in_=asm2[:, (r % 2) * D:(r % 2) * D + D]
                                    .rearrange("p (c o) -> p c o", o=128)
                                    ).then_inc(q_st[r % 2], 16)

                @block.tensor
                def _(t):
                    t.wait_ge(id_ld, 16)
                    def wq_transpose_tile(r):
                        t.wait_ge(q_dq, 32 * (r + 1))
                        for blk in range(NC):
                            b = r * NC + blk
                            if b >= 4:
                                t.wait_ge(wqev, b - 3)
                            t.transpose(ptr[b % 4][:],
                                        wout[:, (r % 2) * D + blk * 128:
                                             (r % 2) * D + blk * 128 + 128],
                                        ident_bf[:]).then_inc(wqtr, 1)
                    seen_sb = set()
                    for qq, (sb, tc) in enumerate(P_PAIRS):
                        if sb not in seen_sb:
                            seen_sb.add(sb)
                            t.wait_ge(w1_ld, 64 * (sb + 1))
                        t.wait_ge(wt_ld[qq % 2], 16 * (qq // 2 + 1))
                        for k in PAIR_TILES[qq]:
                            st = K_TILES[k][2]
                            if k >= 2:
                                t.wait_ge(p1_ev, k - 1)
                            for dc in range(NC):
                                ins = t.matmul(
                                    pa[k % 2][:],
                                    w1blk[:, dc * 1024 + st * 128:
                                          dc * 1024 + st * 128 + 128],
                                    w2t[:, (qq % 2) * NC * 512 + dc * 512:
                                        (qq % 2) * NC * 512 + dc * 512 + 512],
                                    start=(dc == 0), stop=(dc == NC - 1),
                                )
                            ins.then_inc(p1_pe, 1)
                            if 8 <= k < 40:
                                wq_transpose_tile(k - 8)

            with _ES() as es_2:
                x_sb = es_2.enter_context(nc.sbuf_tensor("x_sb", [128, NT * D], F32))
                hT = es_2.enter_context(nc.sbuf_tensor("hT", [128, NC * 512], BF16))
                mv = es_2.enter_context(nc.sbuf_tensor("mv", [128, 2 * NC * 512], BF16))
                h_tmp = es_2.enter_context(nc.sbuf_tensor("h_tmp", [128, D], BF16))
                sq_scr = es_2.enter_context(nc.sbuf_tensor("sq_scr", [128, D], BF16))
                gel = es_2.enter_context(nc.sbuf_tensor("gel", [128, 2 * 512], F32))
                psb = [es_2.enter_context(nc.psum_tensor(f"ps{i}", [128, 512], F32))
                       for i in range(4)]
                trp = [es_2.enter_context(nc.psum_tensor(f"tr{i}", [128, 128], BF16))
                       for i in range(4)]

                @block.sync
                def _(sync):
                    sync.wait_ge(p1_pe, NK)
                    sync.wait_ge(p1_ev, NK)
                    sync.wait_ge(p1_st[0], 16 * ((NK + 1) // 2))
                    sync.wait_ge(p1_st[1], 16 * (NK // 2))
                    sync.wait_ge(q_dq, 1024)
                    sync.wait_ge(q_st[0], 256)
                    sync.wait_ge(q_st[1], 256)

                    for u in range(2):
                        if u == 1:
                            sync.wait_ge(st_out, 64)
                        for j in range(NT):
                            sync.dma_start(
                                out=x_sb[:, j * D:(j + 1) * D],
                                in_=x_in[u * 512 + j * 128:
                                         u * 512 + (j + 1) * 128, :],
                            ).then_inc(x_ld, 16)
                        for p in range(3):
                            for oc in range(8):
                                g = G_base(u, p) + oc
                                if g >= 2:
                                    sync.wait_ge(p2_pe, g - 1)
                                sl = (g % 2) * NC * 512
                                if p == 1:
                                    S0, nk = ATT_SC[oc][0], len(ATT_SC[oc])
                                    sync.dma_start(
                                        out=mv[:, sl:sl + nk * 512]
                                            .rearrange("p (c t) -> p c t", t=512),
                                        in_=amat.rearrange("(c p) t -> p c t",
                                                           p=128)
                                            [:, S0:S0 + nk,
                                             oc * 512:(oc + 1) * 512],
                                    ).then_inc(mv_ld[g % 2], 16)
                                else:
                                    if (u, p, oc) == (0, 0, 0):
                                        sync.wait_ge(q_st[0], 256)
                                        sync.wait_ge(q_st[1], 256)
                                    sync.dma_start(
                                        out=mv[:, sl:sl + NC * 512]
                                            .rearrange("p (c t) -> p c t", t=512),
                                        in_=wdqT.rearrange("(c p) o -> p c o",
                                                           p=128)
                                            [:, :, oc * 512:(oc + 1) * 512],
                                    ).then_inc(mv_ld[g % 2], 16)
                        sync.wait_ge(p2_ev, (u * 3 + 3) * 32)
                        for j in range(NT):
                            sync.dma_start(
                                out=y_out[u * 512 + j * 128:
                                          u * 512 + (j + 1) * 128, :],
                                in_=x_sb[:, j * D:(j + 1) * D],
                            ).then_inc(st_out, 16)
                    sync.wait_ge(st_out, 128)

                @block.scalar
                def _(s):
                    for u in range(2):
                        for p in range(3):
                            for j in range(NT):
                                a0 = A_base(u, p, j)
                                v0 = V_base(u, p, j)
                                s.wait_ge(ac_ln, a0)
                                if p == 0:
                                    s.wait_ge(x_ld, 64 * (u + 1))
                                    if u == 1:
                                        s.wait_ge(p2_ev, 96)
                                    s.activation(sq_scr[:], x_sb[:, j * D:(j + 1) * D],
                                                 AF.Copy,
                                                 accum_out=stats[:, j:j + 1]
                                                 ).then_inc(ac_ln, 1)
                                    s.wait_ge(ac_ln, a0 + 1)
                                else:
                                    s.wait_ge(p2_ev, E_base(u, p - 1) + 29 + j)
                                s.activation(sq_scr[:], x_sb[:, j * D:(j + 1) * D],
                                             AF.Square,
                                             accum_out=stats[:, 4 + j:5 + j]
                                             ).then_inc(ac_ln, 1)
                                tvar_idx = 3 if p == 0 else 4
                                s.wait_ge(ve_ln, v0 + tvar_idx)
                                s.activation(stats[:, 16 + j:17 + j],
                                             stats[:, 4 + j:5 + j],
                                             AF.Sqrt,
                                             bias=stats[:, 12 + j:13 + j],
                                             scale=1.0 / D).then_inc(ac_ln, 1)
                            if p == 1:
                                for oc in range(8):
                                    for j in range(NT):
                                        l = oc * 4 + j
                                        s.wait_ge(p2_pe, G_base(u, 1) + oc + 1)
                                        if l >= 2:
                                            s.wait_ge(p2_ev, E_base(u, 1) + l - 1)
                                        s.activation(
                                            gel[:, (l % 2) * 512:(l % 2) * 512 + 512],
                                            psb[j][:],
                                            AF.Gelu).then_inc(p2_gl, 1)

                @block.tensor
                def _(t):
                    t.wait_ge(p1_ev, NK)
                    t.wait_ge(ms_ld, 64)
                    t.wait_ge(ms, 2)
                    for u in range(2):
                        for p in range(3):
                            for j in range(NT):
                                t.wait_ge(ve_ln, V_base(u, p, j) + cV(p))
                                for ic in range(NC):
                                    tg = tr_base(u, p) + j * 32 + ic
                                    if tg >= 4:
                                        t.wait_ge(p2_trv, tg - 3)
                                    t.transpose(trp[tg % 4][:],
                                                h_tmp[:, ic * 128:(ic + 1) * 128],
                                                ident_bf[:]).then_inc(p2_tr, 1)
                            for oc in range(8):
                                g = G_base(u, p) + oc
                                if oc == 0:
                                    t.wait_ge(p2_trv, tr_base(u, p) + 128)
                                t.wait_ge(mv_ld[g % 2], mv_slot_done(u, p, oc))
                                t.wait_ge(p2_ev, E_base(u, p) + 4 * oc)
                                if p == 1:
                                    t.wait_ge(p2_gl, u * 32 + 4 * oc)
                                if p == 2:
                                    t.wait_ge(p2_gl, u * 32 + 32)
                                sl = (g % 2) * NC * 512
                                if p == 1:
                                    chunks = ATT_SC[oc]
                                else:
                                    chunks = list(range(NC))
                                for ii, ic in enumerate(chunks):
                                    for j in range(NT):
                                        t.matmul(
                                            psb[j][:],
                                            hT[:, ic * 512 + j * 128:
                                               ic * 512 + j * 128 + 128],
                                            mv[:, sl + ii * 512:sl + ii * 512 + 512],
                                            start=(ii == 0), stop=False,
                                        )
                                brow = b2_bf if p == 1 else fb_bf
                                lrow = ones_bf if p == 1 else halves_bf
                                for j in range(NT):
                                    ins = t.matmul(
                                        psb[j][:],
                                        lrow[0:1, :],
                                        brow[0:1, oc * 512:(oc + 1) * 512],
                                        start=False, stop=True,
                                    )
                                ins.then_inc(p2_pe, 1)

                @block.vector
                def _(v):
                    for u in range(2):
                        for p in range(3):
                            for j in range(NT):
                                a0 = A_base(u, p, j)
                                v0 = V_base(u, p, j)
                                if p > 0:
                                    v.wait_ge(p2_ev, E_base(u, p - 1) + 29 + j)
                                    v.tensor_reduce(stats[:, j:j + 1],
                                                    sumcols[:, j * 8:j * 8 + 8],
                                                    axis=AX.X, op=ALU.add
                                                    ).then_inc(ve_ln, 1)
                                m_idx = 1 if p == 0 else 2
                                v.wait_ge(ac_ln, a0 + 2 if p == 0 else a0 + 1)
                                if p != 0:
                                    v.wait_ge(ve_ln, v0 + 1)
                                v.tensor_scalar_mul(stats[:, 8 + j:9 + j],
                                                    stats[:, j:j + 1], 1.0 / D
                                                    ).then_inc(ve_ln, 1)
                                v.wait_ge(ve_ln, v0 + m_idx)
                                v.tensor_mul(stats[:, 12 + j:13 + j],
                                             stats[:, 8 + j:9 + j],
                                             stats[:, 8 + j:9 + j]
                                             ).then_inc(ve_ln, 1)
                                v.wait_ge(ve_ln, v0 + m_idx + 1)
                                v.tensor_scalar(stats[:, 12 + j:13 + j],
                                                stats[:, 12 + j:13 + j],
                                                -1.0, EPS,
                                                op0=ALU.mult, op1=ALU.add
                                                ).then_inc(ve_ln, 1)
                                v.wait_ge(ac_ln, a0 + cA(p))
                                v.reciprocal(stats[:, 20 + j:21 + j],
                                             stats[:, 16 + j:17 + j]
                                             ).then_inc(ve_ln, 1)
                                v.wait_ge(ve_ln, v0 + m_idx + 3)
                                v.wait_ge(p2_tr, tr_base(u, p) + j * 32)
                                v.tensor_scalar(h_tmp[:],
                                                x_sb[:, j * D:(j + 1) * D],
                                                stats[:, 8 + j:9 + j],
                                                stats[:, 20 + j:21 + j],
                                                op0=ALU.subtract, op1=ALU.mult
                                                ).then_inc(ve_ln, 1)
                                if j == 0:
                                    v.wait_ge(p2_pe, G_base(u, p))
                                for ic in range(NC):
                                    tg = tr_base(u, p) + j * 32 + ic
                                    v.wait_ge(p2_tr, tg + 1)
                                    v.tensor_copy(
                                        hT[:, ic * 512 + j * 128:
                                           ic * 512 + j * 128 + 128],
                                        trp[tg % 4][:],
                                    ).then_inc(p2_trv, 1)
                            for oc in range(8):
                                for j in range(NT):
                                    v.wait_ge(p2_pe, G_base(u, p) + oc + 1)
                                    if p == 1:
                                        l = oc * 4 + j
                                        v.wait_ge(p2_gl, u * 32 + l + 1)
                                        src = gel[:, (l % 2) * 512:(l % 2) * 512 + 512]
                                    else:
                                        src = psb[j][:]
                                    v.scalar_tensor_tensor(
                                        x_sb[:, j * D + oc * 512:
                                             j * D + oc * 512 + 512],
                                        src, 1.0,
                                        x_sb[:, j * D + oc * 512:
                                             j * D + oc * 512 + 512],
                                        op0=ALU.mult, op1=ALU.add,
                                        accum_out=sumcols[:, j * 8 + oc:
                                                          j * 8 + oc + 1],
                                    ).then_inc(p2_ev, 1)
    return nc


def kernel(L0_attn_x, L0_attn_w1, L0_attn_w2, L0_attn_b2, L0_attn_sparse_mask,
           L0_ffn_qweight, L0_ffn_scales, L0_ffn_scaled_zeros, L0_ffn_bias):
    x = np.ascontiguousarray(np.asarray(L0_attn_x, dtype=np.float32).reshape(8192, D))
    w1 = np.ascontiguousarray(np.asarray(L0_attn_w1, dtype=np.float32))
    w2T = np.ascontiguousarray(np.asarray(L0_attn_w2, dtype=np.float32).T)
    b2 = np.ascontiguousarray(np.asarray(L0_attn_b2, dtype=np.float32).reshape(1, D))
    mask = np.ascontiguousarray(np.asarray(L0_attn_sparse_mask, dtype=np.float32))
    qw = np.ascontiguousarray(np.asarray(L0_ffn_qweight, dtype=np.int32).reshape(D, D))
    scal = np.ascontiguousarray(np.asarray(L0_ffn_scales, dtype=np.float32).reshape(D, 32))
    zer = np.ascontiguousarray(np.asarray(L0_ffn_scaled_zeros, dtype=np.float32).reshape(D, 32))
    fb = np.ascontiguousarray(np.asarray(L0_ffn_bias, dtype=np.float32).reshape(1, D))
    ident = np.eye(128, dtype=np.float32)

    nc = build_nc()
    in_maps = []
    for c in range(8):
        in_maps.append({
            "x_in": x[c * TCORE:(c + 1) * TCORE],
            "w1": w1, "w2T": w2T, "mask": mask, "b2": b2,
            "qw": qw, "scal": scal, "zer": zer, "fb": fb, "ident": ident,
        })
    res = run_bass_kernel_spmd(nc, in_maps, core_ids=list(range(8)))
    out = np.concatenate([res.results[c]["y_out"] for c in range(8)], axis=0)
    return out.reshape(1, 8192, D).astype(np.float32)


# version 22
# speedup vs baseline: 1.0593x; 1.0593x over previous
import numpy as np
import concourse.bass as bass
import concourse.mybir as mybir
from concourse.bass_utils import run_bass_kernel_spmd

F32 = mybir.dt.float32
BF16 = mybir.dt.bfloat16
I32 = mybir.dt.int32
AF = mybir.ActivationFunctionType
ALU = mybir.AluOpType
AX = mybir.AxisListType

D = 4096
TCORE = 1024
NT = 4
NC = 32
EPS = 1e-5

def cA(p):
    return 3 if p == 0 else 2

def cV(p):
    return 5 if p == 0 else 6

def A_base(u, p, j):
    return u * 28 + sum(4 * cA(q) for q in range(p)) + j * cA(p)

def V_base(u, p, j):
    return u * 68 + sum(4 * cV(q) for q in range(p)) + j * cV(p)

def tr_base(u, p):
    return (u * 3 + p) * 128

def G_base(u, p):
    return (u * 3 + p) * 8

def E_base(u, p):
    return (u * 3 + p) * 32

import math as _math

def _mask_support_tiles():
    n, window, stripes = D, 6, 3
    m = np.zeros((n, n), dtype=bool)
    stride = int(_math.sqrt(n))
    for k in range(stripes):
        ps = stride * k
        for i in range(0, n, window):
            m[ps + i:ps + i + window, i:i + window] = True
    m |= m.T
    return [[bool(m[S * 128:(S + 1) * 128, tc * 512:(tc + 1) * 512].any())
             for tc in range(8)] for S in range(32)]

KEEP = _mask_support_tiles()
K_TILES = []
P_PAIRS = []
PAIR_TILES = []
for _sb in range(4):
    for _tc in range(8):
        _sts = [st for st in range(8) if KEEP[_sb * 8 + st][_tc]]
        if _sts:
            P_PAIRS.append((_sb, _tc))
            PAIR_TILES.append(list(range(len(K_TILES), len(K_TILES) + len(_sts))))
            for st in _sts:
                K_TILES.append((_sb, _tc, st, _sb * 8 + st))
NK = len(K_TILES)
NP = len(P_PAIRS)
K_BEFORE = [PAIR_TILES[qq][0] for qq in range(NP)] + [NK]
SB_FIRST = {}
for qq, (sb, tc) in enumerate(P_PAIRS):
    SB_FIRST.setdefault(sb, qq)
TC_FIRST = []
_seen = set()
for (sb, tc) in P_PAIRS:
    TC_FIRST.append(tc not in _seen)
    _seen.add(tc)
TC_STORE_IDX = {}
_n = 0
for qq, (sb, tc) in enumerate(P_PAIRS):
    if TC_FIRST[qq]:
        TC_STORE_IDX[tc] = _n
        _n += 1
STORES_THRU = []
_n = 0
for qq in range(NP):
    if TC_FIRST[qq]:
        _n += 1
    STORES_THRU.append(_n)
WT_DONE = []
_tot = [0, 0]
for qq in range(NP):
    _tot[qq % 2] += 512 if TC_FIRST[qq] else 16
    WT_DONE.append(_tot[qq % 2])
WT_CNT = []
_fc = {}
for _qq in range(NP):
    _key = (TC_FIRST[_qq], _qq % 2)
    _fc[_key] = _fc.get(_key, 0) + 16
    WT_CNT.append(_fc[_key])

ATT_SC = [[S for S in range(32) if KEEP[S][oc]] for oc in range(8)]

def mv_slot_done(u, p, oc):
    g = (u * 3 + p) * 8 + oc
    return 16 * (g // 2 + 1)


def build_nc():
    nc = bass.Bass("TRN2", target_bir_lowering=False)

    x_in = nc.dram_tensor("x_in", [TCORE, D], F32, kind="ExternalInput").ap()
    w1 = nc.dram_tensor("w1", [D, D], F32, kind="ExternalInput").ap()
    w2T = nc.dram_tensor("w2T", [D, D], F32, kind="ExternalInput").ap()
    mask = nc.dram_tensor("mask", [D, D], F32, kind="ExternalInput").ap()
    b2 = nc.dram_tensor("b2", [1, D], F32, kind="ExternalInput").ap()
    qw = nc.dram_tensor("qw", [D, D], I32, kind="ExternalInput").ap()
    scal = nc.dram_tensor("scal", [D, 32], F32, kind="ExternalInput").ap()
    zer = nc.dram_tensor("zer", [D, 32], F32, kind="ExternalInput").ap()
    fb = nc.dram_tensor("fb", [1, D], F32, kind="ExternalInput").ap()
    ident_in = nc.dram_tensor("ident", [128, 128], F32, kind="ExternalInput").ap()
    y_out = nc.dram_tensor("y_out", [TCORE, D], F32, kind="ExternalOutput").ap()

    w2b = nc.dram_tensor("w2b", [D, D], BF16)
    amat = nc.dram_tensor("amat", [D, D], BF16)
    wdqT = nc.dram_tensor("wdqT", [D, D], BF16)

    from contextlib import ExitStack
    es = ExitStack()

    def sem1(name):
        return es.enter_context(nc.semaphore(name))

    def semn(name, n):
        return [es.enter_context(nc.semaphore(f"{name}{i}")) for i in range(n)]

    ms = sem1("ms")
    q_dq = sem1("q_dq")
    p1_pe = sem1("p1_pe")
    p1_ev = sem1("p1_ev")
    p2_tr = sem1("p2_tr")
    p2_trv = sem1("p2_trv")
    p2_pe = sem1("p2_pe")
    p2_ev = sem1("p2_ev")
    p2_gl = sem1("p2_gl")
    ac_ln = sem1("ac_ln")
    ve_ln = sem1("ve_ln")
    ms_ld = sem1("ms_ld")
    id_ld = sem1("id_ld")
    wqtr = sem1("wqtr")
    wqev = sem1("wqev")
    q_ld = semn("q_ld", 2)
    q_st = semn("q_st", 2)
    w1_lda = sem1("w1_lda")
    w1_ldb = sem1("w1_ldb")
    wb_st = semn("wb_st", 8)
    wt_ld = semn("wt_ld", 2)
    wr_ld = semn("wr_ld", 2)
    mk_ld = semn("mk_ld", 2)
    p1_st = semn("p1_st", 2)
    x_ld = sem1("x_ld")
    mv_ld = semn("mv_ld", 2)
    st_out = sem1("st_out")

    with es:
        ident_bf = es.enter_context(nc.sbuf_tensor("ident_bf", [128, 128], BF16))
        ones_bf = es.enter_context(nc.sbuf_tensor("ones_bf", [1, 128], BF16))
        fb_bf = es.enter_context(nc.sbuf_tensor("fb_bf", [1, D], BF16))
        halves_bf = es.enter_context(nc.sbuf_tensor("halves_bf", [1, 128], BF16))
        b2_bf = es.enter_context(nc.sbuf_tensor("b2_bf", [1, D], BF16))
        stats = es.enter_context(nc.sbuf_tensor("stats", [128, 32], F32))
        sumcols = es.enter_context(nc.sbuf_tensor("sumcols", [128, 32], F32))
        with nc.Block() as block:
            from contextlib import ExitStack as _ES

            with _ES() as es_1:
                qt = es_1.enter_context(nc.sbuf_tensor("qt", [128, 2 * D], BF16))
                scl_sb = es_1.enter_context(nc.sbuf_tensor("scl_sb", [128, 1024], F32))
                zro_sb = es_1.enter_context(nc.sbuf_tensor("zro_sb", [128, 1024], F32))
                wout = es_1.enter_context(nc.sbuf_tensor("wout", [128, 2 * D], BF16))
                w1blk = es_1.enter_context(nc.sbuf_tensor("w1blk", [128, NC * 1024], BF16))
                w2t = es_1.enter_context(nc.sbuf_tensor("w2t", [128, 2 * NC * 512], BF16))
                mkt = es_1.enter_context(nc.sbuf_tensor("mkt", [128, 2 * 512], F32))
                asm2 = es_1.enter_context(nc.sbuf_tensor("asm2", [128, 2 * D], BF16))
                aev = es_1.enter_context(nc.sbuf_tensor("aev", [128, 2 * 512], BF16))
                pa = [es_1.enter_context(nc.psum_tensor(f"pa{i}", [128, 512], F32))
                      for i in range(2)]
                ptr = [es_1.enter_context(nc.psum_tensor(f"ptr{i}", [128, 128], BF16))
                       for i in range(4)]

                @block.gpsimd
                def _(g):
                    g.dma_start(out=ident_bf[:], in_=ident_in[:, :]).then_inc(id_ld, 16)
                    w1v = w1.rearrange("(c p) s -> p c s", p=128)
                    for h in range(4):
                        g.dma_start(
                            out=w1blk[:].rearrange("p (c s) -> p c s", s=1024)
                                [:, h * 8:(h + 1) * 8, 0:1024],
                            in_=w1v[:, h * 8:(h + 1) * 8, 0:1024],
                        ).then_inc(w1_lda if h < 2 else w1_ldb, 16)
                    g.dma_start(out=b2_bf[:], in_=b2[:, :]).then_inc(ms_ld, 16)
                    g.dma_start(out=fb_bf[:], in_=fb[:, :]).then_inc(ms_ld, 16)
                    g.dma_start(out=scl_sb[:].rearrange("p (r g) -> p r g", g=32),
                                in_=scal.rearrange("(r p) g -> p r g", p=128)
                                ).then_inc(ms_ld, 16)
                    g.dma_start(out=zro_sb[:].rearrange("p (r g) -> p r g", g=32),
                                in_=zer.rearrange("(r p) g -> p r g", p=128)
                                ).then_inc(ms_ld, 16)
                    def qw_load(r):
                        if r == 0:
                            g.wait_ge(ms, 4)
                        if r >= 2:
                            g.wait_ge(q_dq, 32 * (r - 1))
                        g.dma_start(out=qt[:, (r % 2) * D:(r % 2) * D + D],
                                    in_=qw[r * 128:(r + 1) * 128, :]
                                    ).then_inc(q_ld[r % 2], 16)
                    done_sb = set()
                    for qq, (sb, tc) in enumerate(P_PAIRS):
                        if sb not in done_sb:
                            done_sb.add(sb)
                            if sb >= 1:
                                g.wait_ge(p1_pe, K_BEFORE[SB_FIRST[sb]])
                                for h in range(4):
                                    g.dma_start(
                                        out=w1blk[:]
                                            .rearrange("p (c s) -> p c s", s=1024)
                                            [:, h * 8:(h + 1) * 8, 0:1024],
                                        in_=w1v[:, h * 8:(h + 1) * 8,
                                                sb * 1024:(sb + 1) * 1024],
                                    ).then_inc(w1_lda if h < 2 else w1_ldb, 16)
                        if TC_FIRST[qq]:
                            if qq >= 2:
                                g.wait_ge(p1_pe, K_BEFORE[qq - 1])
                                if TC_FIRST[qq - 2]:
                                    g.wait_ge(wb_st[P_PAIRS[qq - 2][1]], 16)
                            sl2 = (qq % 2) * NC * 512
                            g.dma_start(
                                out=w2t[:, sl2:sl2 + NC * 512]
                                    .rearrange("p (c t) -> p c t", t=512),
                                in_=w2T.rearrange("(c p) t -> p c t", p=128)
                                    [:, :, tc * 512:(tc + 1) * 512],
                            ).then_inc(wt_ld[qq % 2], 16)
                        for k in PAIR_TILES[qq]:
                            Sg = K_TILES[k][3]
                            g.wait_ge(p1_ev, k + 1)
                            g.dma_start(
                                out=amat[Sg * 128:(Sg + 1) * 128,
                                         tc * 512:(tc + 1) * 512],
                                in_=aev[:, (k % 2) * 512:(k % 2) * 512 + 512],
                            ).then_inc(p1_st[k % 2], 16)
                            if 4 <= k < 36:
                                qw_load(k - 4)

                @block.sync
                def _(sync):
                    for qq, (sb, tc) in enumerate(P_PAIRS):
                        if not TC_FIRST[qq]:
                            sync.wait_ge(p1_pe, K_BEFORE[qq - 1])
                            if TC_FIRST[qq - 2]:
                                sync.wait_ge(wb_st[P_PAIRS[qq - 2][1]], 16)
                            sync.wait_ge(wb_st[tc], 16)
                            sl2 = (qq % 2) * NC * 512
                            sync.dma_start(
                                out=w2t[:, sl2:sl2 + NC * 512]
                                    .rearrange("p (c t) -> p c t", t=512),
                                in_=w2b.rearrange("(c p) t -> p c t", p=128)
                                    [:, :, tc * 512:(tc + 1) * 512],
                            ).then_inc(wr_ld[qq % 2], 16)
                        for k in PAIR_TILES[qq]:
                            Sg = K_TILES[k][3]
                            if k >= 2:
                                sync.wait_ge(p1_ev, k - 1)
                            sync.dma_start(
                                out=mkt[:, (k % 2) * 512:(k % 2) * 512 + 512],
                                in_=mask[Sg * 128:(Sg + 1) * 128,
                                         tc * 512:(tc + 1) * 512],
                            ).then_inc(mk_ld[k % 2], 16)
                @block.vector
                def _(v):
                    v.memset(ones_bf[:], 1.0).then_inc(ms, 1)
                    v.memset(halves_bf[:], 0.5).then_inc(ms, 1)
                    v.wait_ge(ms_ld, 64)
                    v.tensor_scalar_mul(scl_sb[:], scl_sb[:], 0.5).then_inc(ms, 1)
                    v.wait_ge(ms, 3)
                    v.tensor_scalar_mul(zro_sb[:], zro_sb[:], 0.5).then_inc(ms, 1)
                    def dequant_tile(r):
                        v.wait_ge(q_ld[r % 2], 16 * (r // 2 + 1))
                        if r >= 2:
                            v.wait_ge(wqtr, 32 * (r - 1))
                        for gg in range(32):
                            sl = r % 2
                            v.tensor_scalar(
                                wout[:, sl * D + gg * 128: sl * D + gg * 128 + 128],
                                qt[:, sl * D + gg * 128: sl * D + gg * 128 + 128],
                                scl_sb[:, r * 32 + gg: r * 32 + gg + 1],
                                zro_sb[:, r * 32 + gg: r * 32 + gg + 1],
                                op0=ALU.mult, op1=ALU.add,
                            ).then_inc(q_dq, 1)
                    for k in range(NK):
                        v.wait_ge(p1_pe, k + 1)
                        v.wait_ge(mk_ld[k % 2], 16 * (k // 2 + 1))
                        if k >= 2:
                            v.wait_ge(p1_st[k % 2], 16 * (k // 2))
                        v.tensor_tensor(
                            aev[:, (k % 2) * 512:(k % 2) * 512 + 512],
                            pa[k % 2][:],
                            mkt[:, (k % 2) * 512:(k % 2) * 512 + 512],
                            op=ALU.mult,
                        ).then_inc(p1_ev, 1)
                        if 8 <= k < 40:
                            r = k - 8
                            if r >= 2:
                                v.wait_ge(q_st[r % 2], 16 * (r // 2))
                            for blk in range(NC):
                                b = r * NC + blk
                                v.wait_ge(wqtr, b + 1)
                                v.tensor_copy(
                                    asm2[:, (r % 2) * D + blk * 128:
                                         (r % 2) * D + blk * 128 + 128],
                                    ptr[b % 4][:],
                                ).then_inc(wqev, 1)
                        if 6 <= k < 38:
                            dequant_tile(k - 6)

                @block.scalar
                def _(s):
                    acts = []
                    for qq, (sb, tc) in enumerate(P_PAIRS):
                        if TC_FIRST[qq]:
                            acts.append((K_BEFORE[qq], 0, qq))
                    for r in range(32):
                        acts.append((r + 8, 1, r))
                    for _, kind, idx in sorted(acts):
                        if kind == 0:
                            qq = idx
                            tc = P_PAIRS[qq][1]
                            sl2 = (qq % 2) * NC * 512
                            s.wait_ge(wt_ld[qq % 2], WT_CNT[qq])
                            s.dma_start(
                                out=w2b.rearrange("(c p) t -> p c t", p=128)
                                    [:, :, tc * 512:(tc + 1) * 512],
                                in_=w2t[:, sl2:sl2 + NC * 512]
                                    .rearrange("p (c t) -> p c t", t=512),
                            ).then_inc(wb_st[tc], 16)
                        else:
                            r = idx
                            s.wait_ge(wqev, 32 * (r + 1))
                            s.dma_start(out=wdqT.rearrange("(c p) o -> p c o", p=128)
                                        [:, :, r * 128:(r + 1) * 128],
                                        in_=asm2[:, (r % 2) * D:(r % 2) * D + D]
                                        .rearrange("p (c o) -> p c o", o=128)
                                        ).then_inc(q_st[r % 2], 16)

                @block.tensor
                def _(t):
                    t.wait_ge(id_ld, 16)
                    def wq_transpose_tile(r):
                        t.wait_ge(q_dq, 32 * (r + 1))
                        for blk in range(NC):
                            b = r * NC + blk
                            if b >= 4:
                                t.wait_ge(wqev, b - 3)
                            t.transpose(ptr[b % 4][:],
                                        wout[:, (r % 2) * D + blk * 128:
                                             (r % 2) * D + blk * 128 + 128],
                                        ident_bf[:]).then_inc(wqtr, 1)
                    seen_sb = set()
                    for qq, (sb, tc) in enumerate(P_PAIRS):
                        new_sb = sb not in seen_sb
                        if new_sb:
                            seen_sb.add(sb)
                            t.wait_ge(w1_lda, 32 * (sb + 1))
                        if TC_FIRST[qq]:
                            t.wait_ge(wt_ld[qq % 2], WT_CNT[qq])
                        else:
                            t.wait_ge(wr_ld[qq % 2], WT_CNT[qq])
                        for k in PAIR_TILES[qq]:
                            st = K_TILES[k][2]
                            if k >= 2:
                                t.wait_ge(p1_ev, k - 1)
                            for dc in range(NC):
                                if new_sb and k == PAIR_TILES[qq][0] and dc == 16:
                                    t.wait_ge(w1_ldb, 32 * (sb + 1))
                                ins = t.matmul(
                                    pa[k % 2][:],
                                    w1blk[:, dc * 1024 + st * 128:
                                          dc * 1024 + st * 128 + 128],
                                    w2t[:, (qq % 2) * NC * 512 + dc * 512:
                                        (qq % 2) * NC * 512 + dc * 512 + 512],
                                    start=(dc == 0), stop=(dc == NC - 1),
                                )
                            ins.then_inc(p1_pe, 1)
                            if 8 <= k < 40:
                                wq_transpose_tile(k - 8)

            with _ES() as es_2:
                x_sb = es_2.enter_context(nc.sbuf_tensor("x_sb", [128, NT * D], F32))
                hT = es_2.enter_context(nc.sbuf_tensor("hT", [128, NC * 512], BF16))
                mv = es_2.enter_context(nc.sbuf_tensor("mv", [128, 2 * NC * 512], BF16))
                h_tmp = es_2.enter_context(nc.sbuf_tensor("h_tmp", [128, D], BF16))
                sq_scr = es_2.enter_context(nc.sbuf_tensor("sq_scr", [128, D], BF16))
                gel = es_2.enter_context(nc.sbuf_tensor("gel", [128, 2 * 512], F32))
                psb = [es_2.enter_context(nc.psum_tensor(f"ps{i}", [128, 512], F32))
                       for i in range(4)]
                trp = [es_2.enter_context(nc.psum_tensor(f"tr{i}", [128, 128], BF16))
                       for i in range(4)]

                @block.sync
                def _(sync):
                    sync.wait_ge(p1_pe, NK)
                    sync.wait_ge(p1_ev, NK)
                    for _tc in range(8):
                        sync.wait_ge(wb_st[_tc], 16)
                    sync.wait_ge(p1_st[0], 16 * ((NK + 1) // 2))
                    sync.wait_ge(p1_st[1], 16 * (NK // 2))
                    sync.wait_ge(q_dq, 1024)
                    sync.wait_ge(q_st[0], 256)
                    sync.wait_ge(q_st[1], 256)

                    for u in range(2):
                        if u == 1:
                            sync.wait_ge(st_out, 64)
                        for j in range(NT):
                            sync.dma_start(
                                out=x_sb[:, j * D:(j + 1) * D],
                                in_=x_in[u * 512 + j * 128:
                                         u * 512 + (j + 1) * 128, :],
                            ).then_inc(x_ld, 16)
                        for p in range(3):
                            for oc in range(8):
                                g = G_base(u, p) + oc
                                if g >= 2:
                                    sync.wait_ge(p2_pe, g - 1)
                                sl = (g % 2) * NC * 512
                                if p == 1:
                                    S0, nk = ATT_SC[oc][0], len(ATT_SC[oc])
                                    sync.dma_start(
                                        out=mv[:, sl:sl + nk * 512]
                                            .rearrange("p (c t) -> p c t", t=512),
                                        in_=amat.rearrange("(c p) t -> p c t",
                                                           p=128)
                                            [:, S0:S0 + nk,
                                             oc * 512:(oc + 1) * 512],
                                    ).then_inc(mv_ld[g % 2], 16)
                                else:
                                    if (u, p, oc) == (0, 0, 0):
                                        sync.wait_ge(q_st[0], 256)
                                        sync.wait_ge(q_st[1], 256)
                                    sync.dma_start(
                                        out=mv[:, sl:sl + NC * 512]
                                            .rearrange("p (c t) -> p c t", t=512),
                                        in_=wdqT.rearrange("(c p) o -> p c o",
                                                           p=128)
                                            [:, :, oc * 512:(oc + 1) * 512],
                                    ).then_inc(mv_ld[g % 2], 16)
                        sync.wait_ge(p2_ev, (u * 3 + 3) * 32)
                        for j in range(NT):
                            sync.dma_start(
                                out=y_out[u * 512 + j * 128:
                                          u * 512 + (j + 1) * 128, :],
                                in_=x_sb[:, j * D:(j + 1) * D],
                            ).then_inc(st_out, 16)
                    sync.wait_ge(st_out, 128)

                @block.scalar
                def _(s):
                    for _tc in range(8):
                        s.wait_ge(wb_st[_tc], 16)
                    for u in range(2):
                        for p in range(3):
                            for j in range(NT):
                                a0 = A_base(u, p, j)
                                v0 = V_base(u, p, j)
                                s.wait_ge(ac_ln, a0)
                                if p == 0:
                                    s.wait_ge(x_ld, 64 * (u + 1))
                                    if u == 1:
                                        s.wait_ge(p2_ev, 96)
                                    s.activation(sq_scr[:], x_sb[:, j * D:(j + 1) * D],
                                                 AF.Copy,
                                                 accum_out=stats[:, j:j + 1]
                                                 ).then_inc(ac_ln, 1)
                                    s.wait_ge(ac_ln, a0 + 1)
                                else:
                                    s.wait_ge(p2_ev, E_base(u, p - 1) + 29 + j)
                                s.activation(sq_scr[:], x_sb[:, j * D:(j + 1) * D],
                                             AF.Square,
                                             accum_out=stats[:, 4 + j:5 + j]
                                             ).then_inc(ac_ln, 1)
                                tvar_idx = 3 if p == 0 else 4
                                s.wait_ge(ve_ln, v0 + tvar_idx)
                                s.activation(stats[:, 16 + j:17 + j],
                                             stats[:, 4 + j:5 + j],
                                             AF.Sqrt,
                                             bias=stats[:, 12 + j:13 + j],
                                             scale=1.0 / D).then_inc(ac_ln, 1)
                            if p == 1:
                                for oc in range(8):
                                    for j in range(NT):
                                        l = oc * 4 + j
                                        s.wait_ge(p2_pe, G_base(u, 1) + oc + 1)
                                        if l >= 2:
                                            s.wait_ge(p2_ev, E_base(u, 1) + l - 1)
                                        s.activation(
                                            gel[:, (l % 2) * 512:(l % 2) * 512 + 512],
                                            psb[j][:],
                                            AF.Gelu).then_inc(p2_gl, 1)

                @block.tensor
                def _(t):
                    t.wait_ge(p1_ev, NK)
                    t.wait_ge(ms_ld, 64)
                    t.wait_ge(ms, 2)
                    for u in range(2):
                        for p in range(3):
                            for j in range(NT):
                                t.wait_ge(ve_ln, V_base(u, p, j) + cV(p))
                                for ic in range(NC):
                                    tg = tr_base(u, p) + j * 32 + ic
                                    if tg >= 4:
                                        t.wait_ge(p2_trv, tg - 3)
                                    t.transpose(trp[tg % 4][:],
                                                h_tmp[:, ic * 128:(ic + 1) * 128],
                                                ident_bf[:]).then_inc(p2_tr, 1)
                            for oc in range(8):
                                g = G_base(u, p) + oc
                                if oc == 0:
                                    t.wait_ge(p2_trv, tr_base(u, p) + 128)
                                t.wait_ge(mv_ld[g % 2], mv_slot_done(u, p, oc))
                                t.wait_ge(p2_ev, E_base(u, p) + 4 * oc)
                                if p == 1:
                                    t.wait_ge(p2_gl, u * 32 + 4 * oc)
                                if p == 2:
                                    t.wait_ge(p2_gl, u * 32 + 32)
                                sl = (g % 2) * NC * 512
                                if p == 1:
                                    chunks = ATT_SC[oc]
                                else:
                                    chunks = list(range(NC))
                                for ii, ic in enumerate(chunks):
                                    for j in range(NT):
                                        t.matmul(
                                            psb[j][:],
                                            hT[:, ic * 512 + j * 128:
                                               ic * 512 + j * 128 + 128],
                                            mv[:, sl + ii * 512:sl + ii * 512 + 512],
                                            start=(ii == 0), stop=False,
                                        )
                                brow = b2_bf if p == 1 else fb_bf
                                lrow = ones_bf if p == 1 else halves_bf
                                for j in range(NT):
                                    ins = t.matmul(
                                        psb[j][:],
                                        lrow[0:1, :],
                                        brow[0:1, oc * 512:(oc + 1) * 512],
                                        start=False, stop=True,
                                    )
                                ins.then_inc(p2_pe, 1)

                @block.vector
                def _(v):
                    for _tc in range(8):
                        v.wait_ge(wb_st[_tc], 16)
                    for u in range(2):
                        for p in range(3):
                            for j in range(NT):
                                a0 = A_base(u, p, j)
                                v0 = V_base(u, p, j)
                                if p > 0:
                                    v.wait_ge(p2_ev, E_base(u, p - 1) + 29 + j)
                                    v.tensor_reduce(stats[:, j:j + 1],
                                                    sumcols[:, j * 8:j * 8 + 8],
                                                    axis=AX.X, op=ALU.add
                                                    ).then_inc(ve_ln, 1)
                                m_idx = 1 if p == 0 else 2
                                v.wait_ge(ac_ln, a0 + 2 if p == 0 else a0 + 1)
                                if p != 0:
                                    v.wait_ge(ve_ln, v0 + 1)
                                v.tensor_scalar_mul(stats[:, 8 + j:9 + j],
                                                    stats[:, j:j + 1], 1.0 / D
                                                    ).then_inc(ve_ln, 1)
                                v.wait_ge(ve_ln, v0 + m_idx)
                                v.tensor_mul(stats[:, 12 + j:13 + j],
                                             stats[:, 8 + j:9 + j],
                                             stats[:, 8 + j:9 + j]
                                             ).then_inc(ve_ln, 1)
                                v.wait_ge(ve_ln, v0 + m_idx + 1)
                                v.tensor_scalar(stats[:, 12 + j:13 + j],
                                                stats[:, 12 + j:13 + j],
                                                -1.0, EPS,
                                                op0=ALU.mult, op1=ALU.add
                                                ).then_inc(ve_ln, 1)
                                v.wait_ge(ac_ln, a0 + cA(p))
                                v.reciprocal(stats[:, 20 + j:21 + j],
                                             stats[:, 16 + j:17 + j]
                                             ).then_inc(ve_ln, 1)
                                v.wait_ge(ve_ln, v0 + m_idx + 3)
                                v.wait_ge(p2_tr, tr_base(u, p) + j * 32)
                                v.tensor_scalar(h_tmp[:],
                                                x_sb[:, j * D:(j + 1) * D],
                                                stats[:, 8 + j:9 + j],
                                                stats[:, 20 + j:21 + j],
                                                op0=ALU.subtract, op1=ALU.mult
                                                ).then_inc(ve_ln, 1)
                                if j == 0:
                                    v.wait_ge(p2_pe, G_base(u, p))
                                for ic in range(NC):
                                    tg = tr_base(u, p) + j * 32 + ic
                                    v.wait_ge(p2_tr, tg + 1)
                                    v.tensor_copy(
                                        hT[:, ic * 512 + j * 128:
                                           ic * 512 + j * 128 + 128],
                                        trp[tg % 4][:],
                                    ).then_inc(p2_trv, 1)
                            for oc in range(8):
                                for j in range(NT):
                                    v.wait_ge(p2_pe, G_base(u, p) + oc + 1)
                                    if p == 1:
                                        l = oc * 4 + j
                                        v.wait_ge(p2_gl, u * 32 + l + 1)
                                        src = gel[:, (l % 2) * 512:(l % 2) * 512 + 512]
                                    else:
                                        src = psb[j][:]
                                    v.scalar_tensor_tensor(
                                        x_sb[:, j * D + oc * 512:
                                             j * D + oc * 512 + 512],
                                        src, 1.0,
                                        x_sb[:, j * D + oc * 512:
                                             j * D + oc * 512 + 512],
                                        op0=ALU.mult, op1=ALU.add,
                                        accum_out=sumcols[:, j * 8 + oc:
                                                          j * 8 + oc + 1],
                                    ).then_inc(p2_ev, 1)
    return nc


def kernel(L0_attn_x, L0_attn_w1, L0_attn_w2, L0_attn_b2, L0_attn_sparse_mask,
           L0_ffn_qweight, L0_ffn_scales, L0_ffn_scaled_zeros, L0_ffn_bias):
    x = np.ascontiguousarray(np.asarray(L0_attn_x, dtype=np.float32).reshape(8192, D))
    w1 = np.ascontiguousarray(np.asarray(L0_attn_w1, dtype=np.float32))
    w2T = np.ascontiguousarray(np.asarray(L0_attn_w2, dtype=np.float32).T)
    b2 = np.ascontiguousarray(np.asarray(L0_attn_b2, dtype=np.float32).reshape(1, D))
    mask = np.ascontiguousarray(np.asarray(L0_attn_sparse_mask, dtype=np.float32))
    qw = np.ascontiguousarray(np.asarray(L0_ffn_qweight, dtype=np.int32).reshape(D, D))
    scal = np.ascontiguousarray(np.asarray(L0_ffn_scales, dtype=np.float32).reshape(D, 32))
    zer = np.ascontiguousarray(np.asarray(L0_ffn_scaled_zeros, dtype=np.float32).reshape(D, 32))
    fb = np.ascontiguousarray(np.asarray(L0_ffn_bias, dtype=np.float32).reshape(1, D))
    ident = np.eye(128, dtype=np.float32)

    nc = build_nc()
    in_maps = []
    for c in range(8):
        in_maps.append({
            "x_in": x[c * TCORE:(c + 1) * TCORE],
            "w1": w1, "w2T": w2T, "mask": mask, "b2": b2,
            "qw": qw, "scal": scal, "zer": zer, "fb": fb, "ident": ident,
        })
    res = run_bass_kernel_spmd(nc, in_maps, core_ids=list(range(8)))
    out = np.concatenate([res.results[c]["y_out"] for c in range(8)], axis=0)
    return out.reshape(1, 8192, D).astype(np.float32)


# version 23
# speedup vs baseline: 1.0639x; 1.0044x over previous
import numpy as np
import concourse.bass as bass
import concourse.mybir as mybir
from concourse.bass_utils import run_bass_kernel_spmd

F32 = mybir.dt.float32
BF16 = mybir.dt.bfloat16
I32 = mybir.dt.int32
AF = mybir.ActivationFunctionType
ALU = mybir.AluOpType
AX = mybir.AxisListType

D = 4096
TCORE = 1024
NT = 4
NC = 32
EPS = 1e-5

def cA(p):
    return 3 if p == 0 else 2

def cV(p):
    return 5 if p == 0 else 6

def A_base(u, p, j):
    return u * 28 + sum(4 * cA(q) for q in range(p)) + j * cA(p)

def V_base(u, p, j):
    return u * 68 + sum(4 * cV(q) for q in range(p)) + j * cV(p)

def tr_base(u, p):
    return (u * 3 + p) * 128

def G_base(u, p):
    return (u * 3 + p) * 8

def E_base(u, p):
    return (u * 3 + p) * 32

import math as _math

def _mask_support_tiles():
    n, window, stripes = D, 6, 3
    m = np.zeros((n, n), dtype=bool)
    stride = int(_math.sqrt(n))
    for k in range(stripes):
        ps = stride * k
        for i in range(0, n, window):
            m[ps + i:ps + i + window, i:i + window] = True
    m |= m.T
    return [[bool(m[S * 128:(S + 1) * 128, tc * 512:(tc + 1) * 512].any())
             for tc in range(8)] for S in range(32)]

KEEP = _mask_support_tiles()
K_TILES = []
P_PAIRS = []
PAIR_TILES = []
for _sb in range(4):
    for _tc in range(8):
        _sts = [st for st in range(8) if KEEP[_sb * 8 + st][_tc]]
        if _sts:
            P_PAIRS.append((_sb, _tc))
            PAIR_TILES.append(list(range(len(K_TILES), len(K_TILES) + len(_sts))))
            for st in _sts:
                K_TILES.append((_sb, _tc, st, _sb * 8 + st))
NK = len(K_TILES)
NP = len(P_PAIRS)
K_BEFORE = [PAIR_TILES[qq][0] for qq in range(NP)] + [NK]
SB_FIRST = {}
for qq, (sb, tc) in enumerate(P_PAIRS):
    SB_FIRST.setdefault(sb, qq)
TC_FIRST = []
_seen = set()
for (sb, tc) in P_PAIRS:
    TC_FIRST.append(tc not in _seen)
    _seen.add(tc)
TC_STORE_IDX = {}
_n = 0
for qq, (sb, tc) in enumerate(P_PAIRS):
    if TC_FIRST[qq]:
        TC_STORE_IDX[tc] = _n
        _n += 1
STORES_THRU = []
_n = 0
for qq in range(NP):
    if TC_FIRST[qq]:
        _n += 1
    STORES_THRU.append(_n)
WT_DONE = []
_tot = [0, 0]
for qq in range(NP):
    _tot[qq % 2] += 512 if TC_FIRST[qq] else 16
    WT_DONE.append(_tot[qq % 2])
WT_CNT = []
_fc = {}
for _qq in range(NP):
    _key = (TC_FIRST[_qq], _qq % 2)
    _fc[_key] = _fc.get(_key, 0) + 16
    WT_CNT.append(_fc[_key])

ATT_SC = [[S for S in range(32) if KEEP[S][oc]] for oc in range(8)]

QW_POS = [4 + (3 * r) // 2 for r in range(32)]
QW_AT = {}
DQ_AT = {}
TR_AT = {}
for _r, _p in enumerate(QW_POS):
    QW_AT.setdefault(_p, []).append(_r)
    DQ_AT.setdefault(_p + 2, []).append(_r)
    TR_AT.setdefault(_p + 4, []).append(_r)


def mv_slot_done(u, p, oc):
    g = (u * 3 + p) * 8 + oc
    return 16 * (g // 2 + 1)


def build_nc():
    nc = bass.Bass("TRN2", target_bir_lowering=False)

    x_in = nc.dram_tensor("x_in", [TCORE, D], F32, kind="ExternalInput").ap()
    w1 = nc.dram_tensor("w1", [D, D], F32, kind="ExternalInput").ap()
    w2T = nc.dram_tensor("w2T", [D, D], F32, kind="ExternalInput").ap()
    mask = nc.dram_tensor("mask", [D, D], F32, kind="ExternalInput").ap()
    b2 = nc.dram_tensor("b2", [1, D], F32, kind="ExternalInput").ap()
    qw = nc.dram_tensor("qw", [D, D], I32, kind="ExternalInput").ap()
    scal = nc.dram_tensor("scal", [D, 32], F32, kind="ExternalInput").ap()
    zer = nc.dram_tensor("zer", [D, 32], F32, kind="ExternalInput").ap()
    fb = nc.dram_tensor("fb", [1, D], F32, kind="ExternalInput").ap()
    ident_in = nc.dram_tensor("ident", [128, 128], F32, kind="ExternalInput").ap()
    y_out = nc.dram_tensor("y_out", [TCORE, D], F32, kind="ExternalOutput").ap()

    w2b = nc.dram_tensor("w2b", [D, D], BF16)
    amat = nc.dram_tensor("amat", [D, D], BF16)
    wdqT = nc.dram_tensor("wdqT", [D, D], BF16)

    from contextlib import ExitStack
    es = ExitStack()

    def sem1(name):
        return es.enter_context(nc.semaphore(name))

    def semn(name, n):
        return [es.enter_context(nc.semaphore(f"{name}{i}")) for i in range(n)]

    ms = sem1("ms")
    q_dq = sem1("q_dq")
    p1_pe = sem1("p1_pe")
    p1_ev = sem1("p1_ev")
    p2_tr = sem1("p2_tr")
    p2_trv = sem1("p2_trv")
    p2_pe = sem1("p2_pe")
    p2_ev = sem1("p2_ev")
    p2_gl = sem1("p2_gl")
    ac_ln = sem1("ac_ln")
    ve_ln = sem1("ve_ln")
    ms_ld = sem1("ms_ld")
    id_ld = sem1("id_ld")
    wqtr = sem1("wqtr")
    wqev = sem1("wqev")
    q_ld = semn("q_ld", 2)
    q_st = semn("q_st", 2)
    w1_lda = sem1("w1_lda")
    w1_ldb = sem1("w1_ldb")
    wb_st = semn("wb_st", 8)
    wt_ld = semn("wt_ld", 2)
    wr_ld = semn("wr_ld", 2)
    mk_ld = semn("mk_ld", 2)
    p1_st = semn("p1_st", 2)
    x_ld = sem1("x_ld")
    x_ld2 = sem1("x_ld2")
    st_o = semn("st_o", 4)
    mv_ld = semn("mv_ld", 2)

    with es:
        ident_bf = es.enter_context(nc.sbuf_tensor("ident_bf", [128, 128], BF16))
        ones_bf = es.enter_context(nc.sbuf_tensor("ones_bf", [1, 128], BF16))
        fb_bf = es.enter_context(nc.sbuf_tensor("fb_bf", [1, D], BF16))
        halves_bf = es.enter_context(nc.sbuf_tensor("halves_bf", [1, 128], BF16))
        b2_bf = es.enter_context(nc.sbuf_tensor("b2_bf", [1, D], BF16))
        stats = es.enter_context(nc.sbuf_tensor("stats", [128, 32], F32))
        sumcols = es.enter_context(nc.sbuf_tensor("sumcols", [128, 32], F32))
        with nc.Block() as block:
            from contextlib import ExitStack as _ES

            with _ES() as es_1:
                qt = es_1.enter_context(nc.sbuf_tensor("qt", [128, 2 * D], BF16))
                scl_sb = es_1.enter_context(nc.sbuf_tensor("scl_sb", [128, 1024], F32))
                zro_sb = es_1.enter_context(nc.sbuf_tensor("zro_sb", [128, 1024], F32))
                wout = es_1.enter_context(nc.sbuf_tensor("wout", [128, 2 * D], BF16))
                w1blk = es_1.enter_context(nc.sbuf_tensor("w1blk", [128, NC * 1024], BF16))
                w2t = es_1.enter_context(nc.sbuf_tensor("w2t", [128, 2 * NC * 512], BF16))
                mkt = es_1.enter_context(nc.sbuf_tensor("mkt", [128, 2 * 512], F32))
                asm2 = es_1.enter_context(nc.sbuf_tensor("asm2", [128, 2 * D], BF16))
                aev = es_1.enter_context(nc.sbuf_tensor("aev", [128, 2 * 512], BF16))
                pa = [es_1.enter_context(nc.psum_tensor(f"pa{i}", [128, 512], F32))
                      for i in range(2)]
                ptr = [es_1.enter_context(nc.psum_tensor(f"ptr{i}", [128, 128], BF16))
                       for i in range(4)]

                @block.gpsimd
                def _(g):
                    g.dma_start(out=ident_bf[:], in_=ident_in[:, :]).then_inc(id_ld, 16)
                    w1v = w1.rearrange("(c p) s -> p c s", p=128)
                    for h in range(4):
                        g.dma_start(
                            out=w1blk[:].rearrange("p (c s) -> p c s", s=1024)
                                [:, h * 8:(h + 1) * 8, 0:1024],
                            in_=w1v[:, h * 8:(h + 1) * 8, 0:1024],
                        ).then_inc(w1_lda if h < 2 else w1_ldb, 16)
                    g.dma_start(out=b2_bf[:], in_=b2[:, :]).then_inc(ms_ld, 16)
                    g.dma_start(out=fb_bf[:], in_=fb[:, :]).then_inc(ms_ld, 16)
                    g.dma_start(out=scl_sb[:].rearrange("p (r g) -> p r g", g=32),
                                in_=scal.rearrange("(r p) g -> p r g", p=128)
                                ).then_inc(ms_ld, 16)
                    g.dma_start(out=zro_sb[:].rearrange("p (r g) -> p r g", g=32),
                                in_=zer.rearrange("(r p) g -> p r g", p=128)
                                ).then_inc(ms_ld, 16)
                    def qw_load(r):
                        if r == 0:
                            g.wait_ge(ms, 4)
                        if r >= 2:
                            g.wait_ge(q_dq, 32 * (r - 1))
                        g.dma_start(out=qt[:, (r % 2) * D:(r % 2) * D + D],
                                    in_=qw[r * 128:(r + 1) * 128, :]
                                    ).then_inc(q_ld[r % 2], 16)
                    done_sb = set()
                    for qq, (sb, tc) in enumerate(P_PAIRS):
                        if sb not in done_sb:
                            done_sb.add(sb)
                            if sb >= 1:
                                g.wait_ge(p1_pe, K_BEFORE[SB_FIRST[sb]])
                                for h in range(4):
                                    g.dma_start(
                                        out=w1blk[:]
                                            .rearrange("p (c s) -> p c s", s=1024)
                                            [:, h * 8:(h + 1) * 8, 0:1024],
                                        in_=w1v[:, h * 8:(h + 1) * 8,
                                                sb * 1024:(sb + 1) * 1024],
                                    ).then_inc(w1_lda if h < 2 else w1_ldb, 16)
                        if TC_FIRST[qq]:
                            if qq >= 2:
                                g.wait_ge(p1_pe, K_BEFORE[qq - 1])
                                if TC_FIRST[qq - 2]:
                                    g.wait_ge(wb_st[P_PAIRS[qq - 2][1]], 16)
                            sl2 = (qq % 2) * NC * 512
                            g.dma_start(
                                out=w2t[:, sl2:sl2 + NC * 512]
                                    .rearrange("p (c t) -> p c t", t=512),
                                in_=w2T.rearrange("(c p) t -> p c t", p=128)
                                    [:, :, tc * 512:(tc + 1) * 512],
                            ).then_inc(wt_ld[qq % 2], 16)
                        for k in PAIR_TILES[qq]:
                            Sg = K_TILES[k][3]
                            g.wait_ge(p1_ev, k + 1)
                            g.dma_start(
                                out=amat[Sg * 128:(Sg + 1) * 128,
                                         tc * 512:(tc + 1) * 512],
                                in_=aev[:, (k % 2) * 512:(k % 2) * 512 + 512],
                            ).then_inc(p1_st[k % 2], 16)
                            for r in QW_AT.get(k, []):
                                qw_load(r)

                @block.sync
                def _(sync):
                    for qq, (sb, tc) in enumerate(P_PAIRS):
                        if not TC_FIRST[qq]:
                            sync.wait_ge(p1_pe, K_BEFORE[qq - 1])
                            if TC_FIRST[qq - 2]:
                                sync.wait_ge(wb_st[P_PAIRS[qq - 2][1]], 16)
                            sync.wait_ge(wb_st[tc], 16)
                            sl2 = (qq % 2) * NC * 512
                            sync.dma_start(
                                out=w2t[:, sl2:sl2 + NC * 512]
                                    .rearrange("p (c t) -> p c t", t=512),
                                in_=w2b.rearrange("(c p) t -> p c t", p=128)
                                    [:, :, tc * 512:(tc + 1) * 512],
                            ).then_inc(wr_ld[qq % 2], 16)
                        for k in PAIR_TILES[qq]:
                            Sg = K_TILES[k][3]
                            if k >= 2:
                                sync.wait_ge(p1_ev, k - 1)
                            sync.dma_start(
                                out=mkt[:, (k % 2) * 512:(k % 2) * 512 + 512],
                                in_=mask[Sg * 128:(Sg + 1) * 128,
                                         tc * 512:(tc + 1) * 512],
                            ).then_inc(mk_ld[k % 2], 16)
                @block.vector
                def _(v):
                    v.memset(ones_bf[:], 1.0).then_inc(ms, 1)
                    v.memset(halves_bf[:], 0.5).then_inc(ms, 1)
                    v.wait_ge(ms_ld, 64)
                    v.tensor_scalar_mul(scl_sb[:], scl_sb[:], 0.5).then_inc(ms, 1)
                    v.wait_ge(ms, 3)
                    v.tensor_scalar_mul(zro_sb[:], zro_sb[:], 0.5).then_inc(ms, 1)
                    def dequant_tile(r):
                        v.wait_ge(q_ld[r % 2], 16 * (r // 2 + 1))
                        if r >= 2:
                            v.wait_ge(wqtr, 32 * (r - 1))
                        for gg in range(32):
                            sl = r % 2
                            v.tensor_scalar(
                                wout[:, sl * D + gg * 128: sl * D + gg * 128 + 128],
                                qt[:, sl * D + gg * 128: sl * D + gg * 128 + 128],
                                scl_sb[:, r * 32 + gg: r * 32 + gg + 1],
                                zro_sb[:, r * 32 + gg: r * 32 + gg + 1],
                                op0=ALU.mult, op1=ALU.add,
                            ).then_inc(q_dq, 1)
                    for k in range(NK):
                        v.wait_ge(p1_pe, k + 1)
                        v.wait_ge(mk_ld[k % 2], 16 * (k // 2 + 1))
                        if k >= 2:
                            v.wait_ge(p1_st[k % 2], 16 * (k // 2))
                        v.tensor_tensor(
                            aev[:, (k % 2) * 512:(k % 2) * 512 + 512],
                            pa[k % 2][:],
                            mkt[:, (k % 2) * 512:(k % 2) * 512 + 512],
                            op=ALU.mult,
                        ).then_inc(p1_ev, 1)
                        for r in TR_AT.get(k, []):
                            if r >= 2:
                                v.wait_ge(q_st[r % 2], 16 * (r // 2))
                            for blk in range(NC):
                                b = r * NC + blk
                                v.wait_ge(wqtr, b + 1)
                                v.tensor_copy(
                                    asm2[:, (r % 2) * D + blk * 128:
                                         (r % 2) * D + blk * 128 + 128],
                                    ptr[b % 4][:],
                                ).then_inc(wqev, 1)
                        for r in DQ_AT.get(k, []):
                            dequant_tile(r)

                @block.scalar
                def _(s):
                    acts = []
                    for qq, (sb, tc) in enumerate(P_PAIRS):
                        if TC_FIRST[qq]:
                            acts.append((K_BEFORE[qq], 0, qq))
                    for r in range(32):
                        acts.append((QW_POS[r] + 4, 1, r))
                    for _, kind, idx in sorted(acts):
                        if kind == 0:
                            qq = idx
                            tc = P_PAIRS[qq][1]
                            sl2 = (qq % 2) * NC * 512
                            s.wait_ge(wt_ld[qq % 2], WT_CNT[qq])
                            s.dma_start(
                                out=w2b.rearrange("(c p) t -> p c t", p=128)
                                    [:, :, tc * 512:(tc + 1) * 512],
                                in_=w2t[:, sl2:sl2 + NC * 512]
                                    .rearrange("p (c t) -> p c t", t=512),
                            ).then_inc(wb_st[tc], 16)
                        else:
                            r = idx
                            s.wait_ge(wqev, 32 * (r + 1))
                            s.dma_start(out=wdqT.rearrange("(c p) o -> p c o", p=128)
                                        [:, :, r * 128:(r + 1) * 128],
                                        in_=asm2[:, (r % 2) * D:(r % 2) * D + D]
                                        .rearrange("p (c o) -> p c o", o=128)
                                        ).then_inc(q_st[r % 2], 16)

                @block.tensor
                def _(t):
                    t.wait_ge(id_ld, 16)
                    def wq_transpose_tile(r):
                        t.wait_ge(q_dq, 32 * (r + 1))
                        for blk in range(NC):
                            b = r * NC + blk
                            if b >= 4:
                                t.wait_ge(wqev, b - 3)
                            t.transpose(ptr[b % 4][:],
                                        wout[:, (r % 2) * D + blk * 128:
                                             (r % 2) * D + blk * 128 + 128],
                                        ident_bf[:]).then_inc(wqtr, 1)
                    seen_sb = set()
                    for qq, (sb, tc) in enumerate(P_PAIRS):
                        new_sb = sb not in seen_sb
                        if new_sb:
                            seen_sb.add(sb)
                            t.wait_ge(w1_lda, 32 * (sb + 1))
                        if TC_FIRST[qq]:
                            t.wait_ge(wt_ld[qq % 2], WT_CNT[qq])
                        else:
                            t.wait_ge(wr_ld[qq % 2], WT_CNT[qq])
                        for k in PAIR_TILES[qq]:
                            st = K_TILES[k][2]
                            if k >= 2:
                                t.wait_ge(p1_ev, k - 1)
                            for dc in range(NC):
                                if new_sb and k == PAIR_TILES[qq][0] and dc == 16:
                                    t.wait_ge(w1_ldb, 32 * (sb + 1))
                                ins = t.matmul(
                                    pa[k % 2][:],
                                    w1blk[:, dc * 1024 + st * 128:
                                          dc * 1024 + st * 128 + 128],
                                    w2t[:, (qq % 2) * NC * 512 + dc * 512:
                                        (qq % 2) * NC * 512 + dc * 512 + 512],
                                    start=(dc == 0), stop=(dc == NC - 1),
                                )
                            ins.then_inc(p1_pe, 1)
                            for r in TR_AT.get(k, []):
                                wq_transpose_tile(r)

            with _ES() as es_2:
                x_sb = es_2.enter_context(nc.sbuf_tensor("x_sb", [128, NT * D], F32))
                hT = es_2.enter_context(nc.sbuf_tensor("hT", [128, NC * 512], BF16))
                mv = es_2.enter_context(nc.sbuf_tensor("mv", [128, 2 * NC * 512], BF16))
                h_tmp = es_2.enter_context(nc.sbuf_tensor("h_tmp", [128, D], BF16))
                sq_scr = es_2.enter_context(nc.sbuf_tensor("sq_scr", [128, D], BF16))
                gel = es_2.enter_context(nc.sbuf_tensor("gel", [128, 2 * 512], F32))
                psb = [es_2.enter_context(nc.psum_tensor(f"ps{i}", [128, 512], F32))
                       for i in range(4)]
                trp = [es_2.enter_context(nc.psum_tensor(f"tr{i}", [128, 128], BF16))
                       for i in range(4)]

                @block.gpsimd
                def _(g):
                    for j in range(NT):
                        g.wait_ge(st_o[j], 16)
                        g.dma_start(
                            out=x_sb[:, j * D:(j + 1) * D],
                            in_=x_in[512 + j * 128:512 + (j + 1) * 128, :],
                        ).then_inc(x_ld2, 16)

                @block.sync
                def _(sync):
                    sync.wait_ge(p1_pe, NK)
                    sync.wait_ge(p1_ev, NK)
                    for _tc in range(8):
                        sync.wait_ge(wb_st[_tc], 16)
                    sync.wait_ge(p1_st[0], 16 * ((NK + 1) // 2))
                    sync.wait_ge(p1_st[1], 16 * (NK // 2))
                    sync.wait_ge(q_dq, 1024)
                    sync.wait_ge(q_st[0], 256)
                    sync.wait_ge(q_st[1], 256)

                    for u in range(2):
                        if u == 0:
                            for j in range(NT):
                                sync.dma_start(
                                    out=x_sb[:, j * D:(j + 1) * D],
                                    in_=x_in[j * 128:(j + 1) * 128, :],
                                ).then_inc(x_ld, 16)
                        for p in range(3):
                            for oc in range(8):
                                g = G_base(u, p) + oc
                                if g >= 2:
                                    sync.wait_ge(p2_pe, g - 1)
                                sl = (g % 2) * NC * 512
                                if p == 1:
                                    S0, nk = ATT_SC[oc][0], len(ATT_SC[oc])
                                    sync.dma_start(
                                        out=mv[:, sl:sl + nk * 512]
                                            .rearrange("p (c t) -> p c t", t=512),
                                        in_=amat.rearrange("(c p) t -> p c t",
                                                           p=128)
                                            [:, S0:S0 + nk,
                                             oc * 512:(oc + 1) * 512],
                                    ).then_inc(mv_ld[g % 2], 16)
                                else:
                                    if (u, p, oc) == (0, 0, 0):
                                        sync.wait_ge(q_st[0], 256)
                                        sync.wait_ge(q_st[1], 256)
                                    sync.dma_start(
                                        out=mv[:, sl:sl + NC * 512]
                                            .rearrange("p (c t) -> p c t", t=512),
                                        in_=wdqT.rearrange("(c p) o -> p c o",
                                                           p=128)
                                            [:, :, oc * 512:(oc + 1) * 512],
                                    ).then_inc(mv_ld[g % 2], 16)
                        for j in range(NT):
                            sync.wait_ge(p2_ev, E_base(u, 2) + 29 + j)
                            sync.dma_start(
                                out=y_out[u * 512 + j * 128:
                                          u * 512 + (j + 1) * 128, :],
                                in_=x_sb[:, j * D:(j + 1) * D],
                            ).then_inc(st_o[j], 16)
                    for j in range(NT):
                        sync.wait_ge(st_o[j], 32)

                @block.scalar
                def _(s):
                    for _tc in range(8):
                        s.wait_ge(wb_st[_tc], 16)
                    for u in range(2):
                        for p in range(3):
                            for j in range(NT):
                                a0 = A_base(u, p, j)
                                v0 = V_base(u, p, j)
                                s.wait_ge(ac_ln, a0)
                                if p == 0:
                                    s.wait_ge(x_ld if u == 0 else x_ld2, 64)
                                    if u == 1:
                                        s.wait_ge(p2_ev, 96)
                                    s.activation(sq_scr[:], x_sb[:, j * D:(j + 1) * D],
                                                 AF.Copy,
                                                 accum_out=stats[:, j:j + 1]
                                                 ).then_inc(ac_ln, 1)
                                    s.wait_ge(ac_ln, a0 + 1)
                                else:
                                    s.wait_ge(p2_ev, E_base(u, p - 1) + 29 + j)
                                s.activation(sq_scr[:], x_sb[:, j * D:(j + 1) * D],
                                             AF.Square,
                                             accum_out=stats[:, 4 + j:5 + j]
                                             ).then_inc(ac_ln, 1)
                                tvar_idx = 3 if p == 0 else 4
                                s.wait_ge(ve_ln, v0 + tvar_idx)
                                s.activation(stats[:, 16 + j:17 + j],
                                             stats[:, 4 + j:5 + j],
                                             AF.Sqrt,
                                             bias=stats[:, 12 + j:13 + j],
                                             scale=1.0 / D).then_inc(ac_ln, 1)
                            if p == 1:
                                for oc in range(8):
                                    for j in range(NT):
                                        l = oc * 4 + j
                                        s.wait_ge(p2_pe, G_base(u, 1) + oc + 1)
                                        if l >= 2:
                                            s.wait_ge(p2_ev, E_base(u, 1) + l - 1)
                                        s.activation(
                                            gel[:, (l % 2) * 512:(l % 2) * 512 + 512],
                                            psb[j][:],
                                            AF.Gelu).then_inc(p2_gl, 1)

                @block.tensor
                def _(t):
                    t.wait_ge(p1_ev, NK)
                    t.wait_ge(ms_ld, 64)
                    t.wait_ge(ms, 2)
                    for u in range(2):
                        for p in range(3):
                            for j in range(NT):
                                t.wait_ge(ve_ln, V_base(u, p, j) + cV(p))
                                for ic in range(NC):
                                    tg = tr_base(u, p) + j * 32 + ic
                                    if tg >= 4:
                                        t.wait_ge(p2_trv, tg - 3)
                                    t.transpose(trp[tg % 4][:],
                                                h_tmp[:, ic * 128:(ic + 1) * 128],
                                                ident_bf[:]).then_inc(p2_tr, 1)
                            for oc in range(8):
                                g = G_base(u, p) + oc
                                if oc == 0:
                                    t.wait_ge(p2_trv, tr_base(u, p) + 128)
                                t.wait_ge(mv_ld[g % 2], mv_slot_done(u, p, oc))
                                t.wait_ge(p2_ev, E_base(u, p) + 4 * oc)
                                if p == 1:
                                    t.wait_ge(p2_gl, u * 32 + 4 * oc)
                                if p == 2:
                                    t.wait_ge(p2_gl, u * 32 + 32)
                                sl = (g % 2) * NC * 512
                                if p == 1:
                                    chunks = ATT_SC[oc]
                                else:
                                    chunks = list(range(NC))
                                for ii, ic in enumerate(chunks):
                                    for j in range(NT):
                                        t.matmul(
                                            psb[j][:],
                                            hT[:, ic * 512 + j * 128:
                                               ic * 512 + j * 128 + 128],
                                            mv[:, sl + ii * 512:sl + ii * 512 + 512],
                                            start=(ii == 0), stop=False,
                                        )
                                brow = b2_bf if p == 1 else fb_bf
                                lrow = ones_bf if p == 1 else halves_bf
                                for j in range(NT):
                                    ins = t.matmul(
                                        psb[j][:],
                                        lrow[0:1, :],
                                        brow[0:1, oc * 512:(oc + 1) * 512],
                                        start=False, stop=True,
                                    )
                                ins.then_inc(p2_pe, 1)

                @block.vector
                def _(v):
                    for _tc in range(8):
                        v.wait_ge(wb_st[_tc], 16)
                    for u in range(2):
                        for p in range(3):
                            for j in range(NT):
                                a0 = A_base(u, p, j)
                                v0 = V_base(u, p, j)
                                if p > 0:
                                    v.wait_ge(p2_ev, E_base(u, p - 1) + 29 + j)
                                    v.tensor_reduce(stats[:, j:j + 1],
                                                    sumcols[:, j * 8:j * 8 + 8],
                                                    axis=AX.X, op=ALU.add
                                                    ).then_inc(ve_ln, 1)
                                m_idx = 1 if p == 0 else 2
                                v.wait_ge(ac_ln, a0 + 2 if p == 0 else a0 + 1)
                                if p != 0:
                                    v.wait_ge(ve_ln, v0 + 1)
                                v.tensor_scalar_mul(stats[:, 8 + j:9 + j],
                                                    stats[:, j:j + 1], 1.0 / D
                                                    ).then_inc(ve_ln, 1)
                                v.wait_ge(ve_ln, v0 + m_idx)
                                v.tensor_mul(stats[:, 12 + j:13 + j],
                                             stats[:, 8 + j:9 + j],
                                             stats[:, 8 + j:9 + j]
                                             ).then_inc(ve_ln, 1)
                                v.wait_ge(ve_ln, v0 + m_idx + 1)
                                v.tensor_scalar(stats[:, 12 + j:13 + j],
                                                stats[:, 12 + j:13 + j],
                                                -1.0, EPS,
                                                op0=ALU.mult, op1=ALU.add
                                                ).then_inc(ve_ln, 1)
                                v.wait_ge(ac_ln, a0 + cA(p))
                                v.reciprocal(stats[:, 20 + j:21 + j],
                                             stats[:, 16 + j:17 + j]
                                             ).then_inc(ve_ln, 1)
                                v.wait_ge(ve_ln, v0 + m_idx + 3)
                                v.wait_ge(p2_tr, tr_base(u, p) + j * 32)
                                v.tensor_scalar(h_tmp[:],
                                                x_sb[:, j * D:(j + 1) * D],
                                                stats[:, 8 + j:9 + j],
                                                stats[:, 20 + j:21 + j],
                                                op0=ALU.subtract, op1=ALU.mult
                                                ).then_inc(ve_ln, 1)
                                if j == 0:
                                    v.wait_ge(p2_pe, G_base(u, p))
                                for ic in range(NC):
                                    tg = tr_base(u, p) + j * 32 + ic
                                    v.wait_ge(p2_tr, tg + 1)
                                    v.tensor_copy(
                                        hT[:, ic * 512 + j * 128:
                                           ic * 512 + j * 128 + 128],
                                        trp[tg % 4][:],
                                    ).then_inc(p2_trv, 1)
                            for oc in range(8):
                                for j in range(NT):
                                    v.wait_ge(p2_pe, G_base(u, p) + oc + 1)
                                    if p == 1:
                                        l = oc * 4 + j
                                        v.wait_ge(p2_gl, u * 32 + l + 1)
                                        src = gel[:, (l % 2) * 512:(l % 2) * 512 + 512]
                                    else:
                                        src = psb[j][:]
                                    v.scalar_tensor_tensor(
                                        x_sb[:, j * D + oc * 512:
                                             j * D + oc * 512 + 512],
                                        src, 1.0,
                                        x_sb[:, j * D + oc * 512:
                                             j * D + oc * 512 + 512],
                                        op0=ALU.mult, op1=ALU.add,
                                        accum_out=sumcols[:, j * 8 + oc:
                                                          j * 8 + oc + 1],
                                    ).then_inc(p2_ev, 1)
    return nc


def kernel(L0_attn_x, L0_attn_w1, L0_attn_w2, L0_attn_b2, L0_attn_sparse_mask,
           L0_ffn_qweight, L0_ffn_scales, L0_ffn_scaled_zeros, L0_ffn_bias):
    x = np.ascontiguousarray(np.asarray(L0_attn_x, dtype=np.float32).reshape(8192, D))
    w1 = np.ascontiguousarray(np.asarray(L0_attn_w1, dtype=np.float32))
    w2T = np.ascontiguousarray(np.asarray(L0_attn_w2, dtype=np.float32).T)
    b2 = np.ascontiguousarray(np.asarray(L0_attn_b2, dtype=np.float32).reshape(1, D))
    mask = np.ascontiguousarray(np.asarray(L0_attn_sparse_mask, dtype=np.float32))
    qw = np.ascontiguousarray(np.asarray(L0_ffn_qweight, dtype=np.int32).reshape(D, D))
    scal = np.ascontiguousarray(np.asarray(L0_ffn_scales, dtype=np.float32).reshape(D, 32))
    zer = np.ascontiguousarray(np.asarray(L0_ffn_scaled_zeros, dtype=np.float32).reshape(D, 32))
    fb = np.ascontiguousarray(np.asarray(L0_ffn_bias, dtype=np.float32).reshape(1, D))
    ident = np.eye(128, dtype=np.float32)

    nc = build_nc()
    in_maps = []
    for c in range(8):
        in_maps.append({
            "x_in": x[c * TCORE:(c + 1) * TCORE],
            "w1": w1, "w2T": w2T, "mask": mask, "b2": b2,
            "qw": qw, "scal": scal, "zer": zer, "fb": fb, "ident": ident,
        })
    res = run_bass_kernel_spmd(nc, in_maps, core_ids=list(range(8)))
    out = np.concatenate([res.results[c]["y_out"] for c in range(8)], axis=0)
    return out.reshape(1, 8192, D).astype(np.float32)


# version 24
# speedup vs baseline: 1.1162x; 1.0492x over previous
import numpy as np
import concourse.bass as bass
import concourse.mybir as mybir
from concourse.bass_utils import run_bass_kernel_spmd

F32 = mybir.dt.float32
BF16 = mybir.dt.bfloat16
I32 = mybir.dt.int32
AF = mybir.ActivationFunctionType
ALU = mybir.AluOpType
AX = mybir.AxisListType

D = 4096
TCORE = 1024
NT = 4
NC = 32
EPS = 1e-5

def cA(p):
    return 3 if p == 0 else 2

def cV(p):
    return 5 if p == 0 else 6

def A_base(u, p, j):
    return u * 28 + sum(4 * cA(q) for q in range(p)) + j * cA(p)

def V_base(u, p, j):
    return u * 68 + sum(4 * cV(q) for q in range(p)) + j * cV(p)

def tr_base(u, p):
    return (u * 3 + p) * 128

def G_base(u, p):
    return (u * 3 + p) * 8

def E_base(u, p):
    return (u * 3 + p) * 32

import math as _math

def _mask_support_tiles():
    n, window, stripes = D, 6, 3
    m = np.zeros((n, n), dtype=bool)
    stride = int(_math.sqrt(n))
    for k in range(stripes):
        ps = stride * k
        for i in range(0, n, window):
            m[ps + i:ps + i + window, i:i + window] = True
    m |= m.T
    return [[bool(m[S * 128:(S + 1) * 128, tc * 512:(tc + 1) * 512].any())
             for tc in range(8)] for S in range(32)]

KEEP = _mask_support_tiles()
K_TILES = []
P_PAIRS = []
PAIR_TILES = []
for _sb in range(4):
    for _tc in range(8):
        _sts = [st for st in range(8) if KEEP[_sb * 8 + st][_tc]]
        if _sts:
            P_PAIRS.append((_sb, _tc))
            PAIR_TILES.append(list(range(len(K_TILES), len(K_TILES) + len(_sts))))
            for st in _sts:
                K_TILES.append((_sb, _tc, st, _sb * 8 + st))
NK = len(K_TILES)
NP = len(P_PAIRS)
K_BEFORE = [PAIR_TILES[qq][0] for qq in range(NP)] + [NK]
SB_FIRST = {}
for qq, (sb, tc) in enumerate(P_PAIRS):
    SB_FIRST.setdefault(sb, qq)
TC_FIRST = []
_seen = set()
for (sb, tc) in P_PAIRS:
    TC_FIRST.append(tc not in _seen)
    _seen.add(tc)
TC_STORE_IDX = {}
_n = 0
for qq, (sb, tc) in enumerate(P_PAIRS):
    if TC_FIRST[qq]:
        TC_STORE_IDX[tc] = _n
        _n += 1
STORES_THRU = []
_n = 0
for qq in range(NP):
    if TC_FIRST[qq]:
        _n += 1
    STORES_THRU.append(_n)
WT_DONE = []
_tot = [0, 0]
for qq in range(NP):
    _tot[qq % 2] += 512 if TC_FIRST[qq] else 16
    WT_DONE.append(_tot[qq % 2])
WT_CNT = []
_fc = {}
for _qq in range(NP):
    _key = (TC_FIRST[_qq], _qq % 2)
    _fc[_key] = _fc.get(_key, 0) + 16
    WT_CNT.append(_fc[_key])

ATT_SC = [[S for S in range(32) if KEEP[S][oc]] for oc in range(8)]

QW_POS = [4 + (3 * r) // 2 for r in range(32)]
QW_AT = {}
DQ_AT = {}
TR_AT = {}
for _r, _p in enumerate(QW_POS):
    QW_AT.setdefault(_p, []).append(_r)
    DQ_AT.setdefault(_p + 2, []).append(_r)
    TR_AT.setdefault(_p + 4, []).append(_r)


def mv_slot_done(u, p, oc):
    g = (u * 3 + p) * 8 + oc
    return 16 * (g // 2 + 1)


def build_nc():
    nc = bass.Bass("TRN2", target_bir_lowering=False)

    x_in = nc.dram_tensor("x_in", [TCORE, D], F32, kind="ExternalInput").ap()
    w1 = nc.dram_tensor("w1", [D, D], F32, kind="ExternalInput").ap()
    w2T = nc.dram_tensor("w2T", [D, D], F32, kind="ExternalInput").ap()
    mask = nc.dram_tensor("mask", [D, D], F32, kind="ExternalInput").ap()
    b2 = nc.dram_tensor("b2", [1, D], F32, kind="ExternalInput").ap()
    qw = nc.dram_tensor("qw", [D, D], I32, kind="ExternalInput").ap()
    scal = nc.dram_tensor("scal", [D, 32], F32, kind="ExternalInput").ap()
    zer = nc.dram_tensor("zer", [D, 32], F32, kind="ExternalInput").ap()
    fb = nc.dram_tensor("fb", [1, D], F32, kind="ExternalInput").ap()
    ident_in = nc.dram_tensor("ident", [128, 128], F32, kind="ExternalInput").ap()
    y_out = nc.dram_tensor("y_out", [TCORE, D], F32, kind="ExternalOutput").ap()

    w2b = nc.dram_tensor("w2b", [D, D], BF16)
    amat = nc.dram_tensor("amat", [D, D], BF16)
    wdqT = nc.dram_tensor("wdqT", [D, D], BF16)

    from contextlib import ExitStack
    es = ExitStack()

    def sem1(name):
        return es.enter_context(nc.semaphore(name))

    def semn(name, n):
        return [es.enter_context(nc.semaphore(f"{name}{i}")) for i in range(n)]

    ms = sem1("ms")
    q_dq = sem1("q_dq")
    p1_pe = sem1("p1_pe")
    p1_ev = sem1("p1_ev")
    p2_tr = sem1("p2_tr")
    p2_trv = sem1("p2_trv")
    p2_pe = sem1("p2_pe")
    p2_ev = sem1("p2_ev")
    p2_gl = sem1("p2_gl")
    ac_ln = sem1("ac_ln")
    ve_ln = sem1("ve_ln")
    ms_ld = sem1("ms_ld")
    id_ld = sem1("id_ld")
    wqtr = sem1("wqtr")
    wqev = sem1("wqev")
    q_ld = semn("q_ld", 2)
    q_st = semn("q_st", 2)
    w1_lda = sem1("w1_lda")
    w1_ldb = sem1("w1_ldb")
    wb_st = semn("wb_st", 8)
    wt_ld = semn("wt_ld", 2)
    wr_ld = semn("wr_ld", 2)
    wtb_ld = semn("wtb_ld", 2)
    wrb_ld = semn("wrb_ld", 2)
    mk_ld = semn("mk_ld", 2)
    p1_st = semn("p1_st", 2)
    x_ld = sem1("x_ld")
    x_l2 = semn("x_l2", 4)
    st_o = semn("st_o", 4)
    mv_ld = semn("mv_ld", 2)

    with es:
        ident_bf = es.enter_context(nc.sbuf_tensor("ident_bf", [128, 128], BF16))
        ones_bf = es.enter_context(nc.sbuf_tensor("ones_bf", [1, 128], BF16))
        fb_bf = es.enter_context(nc.sbuf_tensor("fb_bf", [1, D], BF16))
        halves_bf = es.enter_context(nc.sbuf_tensor("halves_bf", [1, 128], BF16))
        b2_bf = es.enter_context(nc.sbuf_tensor("b2_bf", [1, D], BF16))
        stats = es.enter_context(nc.sbuf_tensor("stats", [128, 32], F32))
        sumcols = es.enter_context(nc.sbuf_tensor("sumcols", [128, 32], F32))
        with nc.Block() as block:
            from contextlib import ExitStack as _ES

            with _ES() as es_1:
                qt = es_1.enter_context(nc.sbuf_tensor("qt", [128, 2 * D], BF16))
                scl_sb = es_1.enter_context(nc.sbuf_tensor("scl_sb", [128, 1024], F32))
                zro_sb = es_1.enter_context(nc.sbuf_tensor("zro_sb", [128, 1024], F32))
                wout = es_1.enter_context(nc.sbuf_tensor("wout", [128, 2 * D], BF16))
                w1blk = es_1.enter_context(nc.sbuf_tensor("w1blk", [128, NC * 1024], BF16))
                w2t = es_1.enter_context(nc.sbuf_tensor("w2t", [128, 2 * NC * 512], BF16))
                mkt = es_1.enter_context(nc.sbuf_tensor("mkt", [128, 2 * 512], F32))
                asm2 = es_1.enter_context(nc.sbuf_tensor("asm2", [128, 2 * D], BF16))
                aev = es_1.enter_context(nc.sbuf_tensor("aev", [128, 2 * 512], BF16))
                pa = [es_1.enter_context(nc.psum_tensor(f"pa{i}", [128, 512], F32))
                      for i in range(2)]
                ptr = [es_1.enter_context(nc.psum_tensor(f"ptr{i}", [128, 128], BF16))
                       for i in range(4)]

                @block.gpsimd
                def _(g):
                    g.dma_start(out=ident_bf[:], in_=ident_in[:, :]).then_inc(id_ld, 16)
                    w1v = w1.rearrange("(c p) s -> p c s", p=128)
                    for h in range(4):
                        g.dma_start(
                            out=w1blk[:].rearrange("p (c s) -> p c s", s=1024)
                                [:, h * 8:(h + 1) * 8, 0:1024],
                            in_=w1v[:, h * 8:(h + 1) * 8, 0:1024],
                        ).then_inc(w1_lda if h < 2 else w1_ldb, 16)
                    g.dma_start(out=b2_bf[:], in_=b2[:, :]).then_inc(ms_ld, 16)
                    g.dma_start(out=fb_bf[:], in_=fb[:, :]).then_inc(ms_ld, 16)
                    g.dma_start(out=scl_sb[:].rearrange("p (r g) -> p r g", g=32),
                                in_=scal.rearrange("(r p) g -> p r g", p=128)
                                ).then_inc(ms_ld, 16)
                    g.dma_start(out=zro_sb[:].rearrange("p (r g) -> p r g", g=32),
                                in_=zer.rearrange("(r p) g -> p r g", p=128)
                                ).then_inc(ms_ld, 16)
                    def qw_load(r):
                        if r == 0:
                            g.wait_ge(ms, 4)
                        if r >= 2:
                            g.wait_ge(q_dq, 32 * (r - 1))
                        g.dma_start(out=qt[:, (r % 2) * D:(r % 2) * D + D],
                                    in_=qw[r * 128:(r + 1) * 128, :]
                                    ).then_inc(q_ld[r % 2], 16)
                    done_sb = set()
                    for qq, (sb, tc) in enumerate(P_PAIRS):
                        if sb not in done_sb:
                            done_sb.add(sb)
                            if sb >= 1:
                                g.wait_ge(p1_pe, K_BEFORE[SB_FIRST[sb]])
                                for h in range(4):
                                    g.dma_start(
                                        out=w1blk[:]
                                            .rearrange("p (c s) -> p c s", s=1024)
                                            [:, h * 8:(h + 1) * 8, 0:1024],
                                        in_=w1v[:, h * 8:(h + 1) * 8,
                                                sb * 1024:(sb + 1) * 1024],
                                    ).then_inc(w1_lda if h < 2 else w1_ldb, 16)
                        if TC_FIRST[qq]:
                            if qq >= 2:
                                g.wait_ge(p1_pe, K_BEFORE[qq - 1])
                                if TC_FIRST[qq - 2]:
                                    g.wait_ge(wb_st[P_PAIRS[qq - 2][1]], 16)
                            sl2 = (qq % 2) * NC * 512
                            w2Tv = w2T.rearrange("(c p) t -> p c t", p=128)
                            g.dma_start(
                                out=w2t[:, sl2:sl2 + 16 * 512]
                                    .rearrange("p (c t) -> p c t", t=512),
                                in_=w2Tv[:, 0:16, tc * 512:(tc + 1) * 512],
                            ).then_inc(wt_ld[qq % 2], 16)
                            g.dma_start(
                                out=w2t[:, sl2 + 16 * 512:sl2 + NC * 512]
                                    .rearrange("p (c t) -> p c t", t=512),
                                in_=w2Tv[:, 16:NC, tc * 512:(tc + 1) * 512],
                            ).then_inc(wtb_ld[qq % 2], 16)
                        for k in PAIR_TILES[qq]:
                            Sg = K_TILES[k][3]
                            g.wait_ge(p1_ev, k + 1)
                            g.dma_start(
                                out=amat[Sg * 128:(Sg + 1) * 128,
                                         tc * 512:(tc + 1) * 512],
                                in_=aev[:, (k % 2) * 512:(k % 2) * 512 + 512],
                            ).then_inc(p1_st[k % 2], 16)
                            for r in QW_AT.get(k, []):
                                qw_load(r)

                @block.sync
                def _(sync):
                    for qq, (sb, tc) in enumerate(P_PAIRS):
                        if not TC_FIRST[qq]:
                            sync.wait_ge(p1_pe, K_BEFORE[qq - 1])
                            if TC_FIRST[qq - 2]:
                                sync.wait_ge(wb_st[P_PAIRS[qq - 2][1]], 16)
                            sync.wait_ge(wb_st[tc], 16)
                            sl2 = (qq % 2) * NC * 512
                            w2bv = w2b.rearrange("(c p) t -> p c t", p=128)
                            sync.dma_start(
                                out=w2t[:, sl2:sl2 + 16 * 512]
                                    .rearrange("p (c t) -> p c t", t=512),
                                in_=w2bv[:, 0:16, tc * 512:(tc + 1) * 512],
                            ).then_inc(wr_ld[qq % 2], 16)
                            sync.dma_start(
                                out=w2t[:, sl2 + 16 * 512:sl2 + NC * 512]
                                    .rearrange("p (c t) -> p c t", t=512),
                                in_=w2bv[:, 16:NC, tc * 512:(tc + 1) * 512],
                            ).then_inc(wrb_ld[qq % 2], 16)
                        for k in PAIR_TILES[qq]:
                            Sg = K_TILES[k][3]
                            if k >= 2:
                                sync.wait_ge(p1_ev, k - 1)
                            sync.dma_start(
                                out=mkt[:, (k % 2) * 512:(k % 2) * 512 + 512],
                                in_=mask[Sg * 128:(Sg + 1) * 128,
                                         tc * 512:(tc + 1) * 512],
                            ).then_inc(mk_ld[k % 2], 16)
                @block.vector
                def _(v):
                    v.memset(ones_bf[:], 1.0).then_inc(ms, 1)
                    v.memset(halves_bf[:], 0.5).then_inc(ms, 1)
                    v.wait_ge(ms_ld, 64)
                    v.tensor_scalar_mul(scl_sb[:], scl_sb[:], 0.5).then_inc(ms, 1)
                    v.wait_ge(ms, 3)
                    v.tensor_scalar_mul(zro_sb[:], zro_sb[:], 0.5).then_inc(ms, 1)
                    def dequant_tile(r):
                        v.wait_ge(q_ld[r % 2], 16 * (r // 2 + 1))
                        if r >= 2:
                            v.wait_ge(wqtr, 32 * (r - 1))
                        for gg in range(32):
                            sl = r % 2
                            v.tensor_scalar(
                                wout[:, sl * D + gg * 128: sl * D + gg * 128 + 128],
                                qt[:, sl * D + gg * 128: sl * D + gg * 128 + 128],
                                scl_sb[:, r * 32 + gg: r * 32 + gg + 1],
                                zro_sb[:, r * 32 + gg: r * 32 + gg + 1],
                                op0=ALU.mult, op1=ALU.add,
                            ).then_inc(q_dq, 1)
                    for k in range(NK):
                        v.wait_ge(p1_pe, k + 1)
                        v.wait_ge(mk_ld[k % 2], 16 * (k // 2 + 1))
                        if k >= 2:
                            v.wait_ge(p1_st[k % 2], 16 * (k // 2))
                        v.tensor_tensor(
                            aev[:, (k % 2) * 512:(k % 2) * 512 + 512],
                            pa[k % 2][:],
                            mkt[:, (k % 2) * 512:(k % 2) * 512 + 512],
                            op=ALU.mult,
                        ).then_inc(p1_ev, 1)
                        for r in TR_AT.get(k, []):
                            if r >= 2:
                                v.wait_ge(q_st[r % 2], 16 * (r // 2))
                            for blk in range(NC):
                                b = r * NC + blk
                                v.wait_ge(wqtr, b + 1)
                                v.tensor_copy(
                                    asm2[:, (r % 2) * D + blk * 128:
                                         (r % 2) * D + blk * 128 + 128],
                                    ptr[b % 4][:],
                                ).then_inc(wqev, 1)
                        for r in DQ_AT.get(k, []):
                            dequant_tile(r)

                @block.scalar
                def _(s):
                    acts = []
                    for qq, (sb, tc) in enumerate(P_PAIRS):
                        if TC_FIRST[qq]:
                            acts.append((K_BEFORE[qq], 0, qq))
                    for r in range(32):
                        acts.append((QW_POS[r] + 4, 1, r))
                    for _, kind, idx in sorted(acts):
                        if kind == 0:
                            qq = idx
                            tc = P_PAIRS[qq][1]
                            sl2 = (qq % 2) * NC * 512
                            s.wait_ge(wt_ld[qq % 2], WT_CNT[qq])
                            s.wait_ge(wtb_ld[qq % 2], WT_CNT[qq])
                            s.dma_start(
                                out=w2b.rearrange("(c p) t -> p c t", p=128)
                                    [:, :, tc * 512:(tc + 1) * 512],
                                in_=w2t[:, sl2:sl2 + NC * 512]
                                    .rearrange("p (c t) -> p c t", t=512),
                            ).then_inc(wb_st[tc], 16)
                        else:
                            r = idx
                            s.wait_ge(wqev, 32 * (r + 1))
                            s.dma_start(out=wdqT.rearrange("(c p) o -> p c o", p=128)
                                        [:, :, r * 128:(r + 1) * 128],
                                        in_=asm2[:, (r % 2) * D:(r % 2) * D + D]
                                        .rearrange("p (c o) -> p c o", o=128)
                                        ).then_inc(q_st[r % 2], 16)

                @block.tensor
                def _(t):
                    t.wait_ge(id_ld, 16)
                    def wq_transpose_tile(r):
                        t.wait_ge(q_dq, 32 * (r + 1))
                        for blk in range(NC):
                            b = r * NC + blk
                            if b >= 4:
                                t.wait_ge(wqev, b - 3)
                            t.transpose(ptr[b % 4][:],
                                        wout[:, (r % 2) * D + blk * 128:
                                             (r % 2) * D + blk * 128 + 128],
                                        ident_bf[:]).then_inc(wqtr, 1)
                    seen_sb = set()
                    for qq, (sb, tc) in enumerate(P_PAIRS):
                        new_sb = sb not in seen_sb
                        if new_sb:
                            seen_sb.add(sb)
                            t.wait_ge(w1_lda, 32 * (sb + 1))
                        if TC_FIRST[qq]:
                            t.wait_ge(wt_ld[qq % 2], WT_CNT[qq])
                        else:
                            t.wait_ge(wr_ld[qq % 2], WT_CNT[qq])
                        for k in PAIR_TILES[qq]:
                            st = K_TILES[k][2]
                            if k >= 2:
                                t.wait_ge(p1_ev, k - 1)
                            for dc in range(NC):
                                if k == PAIR_TILES[qq][0] and dc == 16:
                                    if new_sb:
                                        t.wait_ge(w1_ldb, 32 * (sb + 1))
                                    t.wait_ge(
                                        (wtb_ld if TC_FIRST[qq] else wrb_ld)
                                        [qq % 2], WT_CNT[qq])
                                ins = t.matmul(
                                    pa[k % 2][:],
                                    w1blk[:, dc * 1024 + st * 128:
                                          dc * 1024 + st * 128 + 128],
                                    w2t[:, (qq % 2) * NC * 512 + dc * 512:
                                        (qq % 2) * NC * 512 + dc * 512 + 512],
                                    start=(dc == 0), stop=(dc == NC - 1),
                                )
                            ins.then_inc(p1_pe, 1)
                            for r in TR_AT.get(k, []):
                                wq_transpose_tile(r)

            with _ES() as es_2:
                x_sb = es_2.enter_context(nc.sbuf_tensor("x_sb", [128, NT * D], F32))
                hT = es_2.enter_context(nc.sbuf_tensor("hT", [128, NC * 512], BF16))
                mv = es_2.enter_context(nc.sbuf_tensor("mv", [128, 2 * NC * 512], BF16))
                h_tmp = es_2.enter_context(nc.sbuf_tensor("h_tmp", [128, D], BF16))
                sq_scr = es_2.enter_context(nc.sbuf_tensor("sq_scr", [128, D], BF16))
                gel = es_2.enter_context(nc.sbuf_tensor("gel", [128, 2 * 512], F32))
                psb = [es_2.enter_context(nc.psum_tensor(f"ps{i}", [128, 512], F32))
                       for i in range(4)]
                trp = [es_2.enter_context(nc.psum_tensor(f"tr{i}", [128, 128], BF16))
                       for i in range(4)]

                @block.gpsimd
                def _(g):
                    for j in range(NT):
                        g.wait_ge(st_o[j], 16)
                        g.dma_start(
                            out=x_sb[:, j * D:(j + 1) * D],
                            in_=x_in[512 + j * 128:512 + (j + 1) * 128, :],
                        ).then_inc(x_l2[j], 16)

                @block.sync
                def _(sync):
                    sync.wait_ge(p1_pe, NK)
                    sync.wait_ge(p1_ev, NK)
                    for _tc in range(8):
                        sync.wait_ge(wb_st[_tc], 16)
                    sync.wait_ge(p1_st[0], 16 * ((NK + 1) // 2))
                    sync.wait_ge(p1_st[1], 16 * (NK // 2))
                    sync.wait_ge(q_dq, 1024)
                    sync.wait_ge(q_st[0], 256)
                    sync.wait_ge(q_st[1], 256)

                    for u in range(2):
                        if u == 0:
                            for j in range(NT):
                                sync.dma_start(
                                    out=x_sb[:, j * D:(j + 1) * D],
                                    in_=x_in[j * 128:(j + 1) * 128, :],
                                ).then_inc(x_ld, 16)
                        for p in range(3):
                            for oc in range(8):
                                g = G_base(u, p) + oc
                                if g >= 2:
                                    sync.wait_ge(p2_pe, g - 1)
                                sl = (g % 2) * NC * 512
                                if p == 1:
                                    S0, nk = ATT_SC[oc][0], len(ATT_SC[oc])
                                    sync.dma_start(
                                        out=mv[:, sl:sl + nk * 512]
                                            .rearrange("p (c t) -> p c t", t=512),
                                        in_=amat.rearrange("(c p) t -> p c t",
                                                           p=128)
                                            [:, S0:S0 + nk,
                                             oc * 512:(oc + 1) * 512],
                                    ).then_inc(mv_ld[g % 2], 16)
                                else:
                                    if (u, p, oc) == (0, 0, 0):
                                        sync.wait_ge(q_st[0], 256)
                                        sync.wait_ge(q_st[1], 256)
                                    sync.dma_start(
                                        out=mv[:, sl:sl + NC * 512]
                                            .rearrange("p (c t) -> p c t", t=512),
                                        in_=wdqT.rearrange("(c p) o -> p c o",
                                                           p=128)
                                            [:, :, oc * 512:(oc + 1) * 512],
                                    ).then_inc(mv_ld[g % 2], 16)
                        for j in range(NT):
                            sync.wait_ge(p2_ev, E_base(u, 2) + 29 + j)
                            sync.dma_start(
                                out=y_out[u * 512 + j * 128:
                                          u * 512 + (j + 1) * 128, :],
                                in_=x_sb[:, j * D:(j + 1) * D],
                            ).then_inc(st_o[j], 16)
                    for j in range(NT):
                        sync.wait_ge(st_o[j], 32)

                @block.scalar
                def _(s):
                    for _tc in range(8):
                        s.wait_ge(wb_st[_tc], 16)
                    for u in range(2):
                        for p in range(3):
                            for j in range(NT):
                                a0 = A_base(u, p, j)
                                v0 = V_base(u, p, j)
                                s.wait_ge(ac_ln, a0)
                                if p == 0:
                                    if u == 0:
                                        s.wait_ge(x_ld, 64)
                                    else:
                                        s.wait_ge(x_l2[j], 16)
                                    if u == 1:
                                        s.wait_ge(p2_ev, 96)
                                    s.activation(sq_scr[:], x_sb[:, j * D:(j + 1) * D],
                                                 AF.Copy,
                                                 accum_out=stats[:, j:j + 1]
                                                 ).then_inc(ac_ln, 1)
                                    s.wait_ge(ac_ln, a0 + 1)
                                else:
                                    s.wait_ge(p2_ev, E_base(u, p - 1) + 29 + j)
                                s.activation(sq_scr[:], x_sb[:, j * D:(j + 1) * D],
                                             AF.Square,
                                             accum_out=stats[:, 4 + j:5 + j]
                                             ).then_inc(ac_ln, 1)
                                tvar_idx = 3 if p == 0 else 4
                                s.wait_ge(ve_ln, v0 + tvar_idx)
                                s.activation(stats[:, 16 + j:17 + j],
                                             stats[:, 4 + j:5 + j],
                                             AF.Sqrt,
                                             bias=stats[:, 12 + j:13 + j],
                                             scale=1.0 / D).then_inc(ac_ln, 1)
                            if p == 1:
                                for oc in range(8):
                                    for j in range(NT):
                                        l = oc * 4 + j
                                        s.wait_ge(p2_pe, G_base(u, 1) + oc + 1)
                                        if l >= 2:
                                            s.wait_ge(p2_ev, E_base(u, 1) + l - 1)
                                        s.activation(
                                            gel[:, (l % 2) * 512:(l % 2) * 512 + 512],
                                            psb[j][:],
                                            AF.Gelu).then_inc(p2_gl, 1)

                @block.tensor
                def _(t):
                    t.wait_ge(p1_ev, NK)
                    t.wait_ge(ms_ld, 64)
                    t.wait_ge(ms, 2)
                    for u in range(2):
                        for p in range(3):
                            for j in range(NT):
                                t.wait_ge(ve_ln, V_base(u, p, j) + cV(p))
                                for ic in range(NC):
                                    tg = tr_base(u, p) + j * 32 + ic
                                    if tg >= 4:
                                        t.wait_ge(p2_trv, tg - 3)
                                    t.transpose(trp[tg % 4][:],
                                                h_tmp[:, ic * 128:(ic + 1) * 128],
                                                ident_bf[:]).then_inc(p2_tr, 1)
                            for oc in range(8):
                                g = G_base(u, p) + oc
                                if oc == 0:
                                    t.wait_ge(p2_trv, tr_base(u, p) + 128)
                                t.wait_ge(mv_ld[g % 2], mv_slot_done(u, p, oc))
                                t.wait_ge(p2_ev, E_base(u, p) + 4 * oc)
                                if p == 1:
                                    t.wait_ge(p2_gl, u * 32 + 4 * oc)
                                if p == 2:
                                    t.wait_ge(p2_gl, u * 32 + 32)
                                sl = (g % 2) * NC * 512
                                if p == 1:
                                    chunks = ATT_SC[oc]
                                else:
                                    chunks = list(range(NC))
                                for ii, ic in enumerate(chunks):
                                    for j in range(NT):
                                        t.matmul(
                                            psb[j][:],
                                            hT[:, ic * 512 + j * 128:
                                               ic * 512 + j * 128 + 128],
                                            mv[:, sl + ii * 512:sl + ii * 512 + 512],
                                            start=(ii == 0), stop=False,
                                        )
                                brow = b2_bf if p == 1 else fb_bf
                                lrow = ones_bf if p == 1 else halves_bf
                                for j in range(NT):
                                    ins = t.matmul(
                                        psb[j][:],
                                        lrow[0:1, :],
                                        brow[0:1, oc * 512:(oc + 1) * 512],
                                        start=False, stop=True,
                                    )
                                ins.then_inc(p2_pe, 1)

                @block.vector
                def _(v):
                    for _tc in range(8):
                        v.wait_ge(wb_st[_tc], 16)
                    for u in range(2):
                        for p in range(3):
                            for j in range(NT):
                                a0 = A_base(u, p, j)
                                v0 = V_base(u, p, j)
                                if p > 0:
                                    v.wait_ge(p2_ev, E_base(u, p - 1) + 29 + j)
                                    v.tensor_reduce(stats[:, j:j + 1],
                                                    sumcols[:, j * 8:j * 8 + 8],
                                                    axis=AX.X, op=ALU.add
                                                    ).then_inc(ve_ln, 1)
                                m_idx = 1 if p == 0 else 2
                                v.wait_ge(ac_ln, a0 + 2 if p == 0 else a0 + 1)
                                if p != 0:
                                    v.wait_ge(ve_ln, v0 + 1)
                                v.tensor_scalar_mul(stats[:, 8 + j:9 + j],
                                                    stats[:, j:j + 1], 1.0 / D
                                                    ).then_inc(ve_ln, 1)
                                v.wait_ge(ve_ln, v0 + m_idx)
                                v.tensor_mul(stats[:, 12 + j:13 + j],
                                             stats[:, 8 + j:9 + j],
                                             stats[:, 8 + j:9 + j]
                                             ).then_inc(ve_ln, 1)
                                v.wait_ge(ve_ln, v0 + m_idx + 1)
                                v.tensor_scalar(stats[:, 12 + j:13 + j],
                                                stats[:, 12 + j:13 + j],
                                                -1.0, EPS,
                                                op0=ALU.mult, op1=ALU.add
                                                ).then_inc(ve_ln, 1)
                                v.wait_ge(ac_ln, a0 + cA(p))
                                v.reciprocal(stats[:, 20 + j:21 + j],
                                             stats[:, 16 + j:17 + j]
                                             ).then_inc(ve_ln, 1)
                                v.wait_ge(ve_ln, v0 + m_idx + 3)
                                v.wait_ge(p2_tr, tr_base(u, p) + j * 32)
                                v.tensor_scalar(h_tmp[:],
                                                x_sb[:, j * D:(j + 1) * D],
                                                stats[:, 8 + j:9 + j],
                                                stats[:, 20 + j:21 + j],
                                                op0=ALU.subtract, op1=ALU.mult
                                                ).then_inc(ve_ln, 1)
                                if j == 0:
                                    v.wait_ge(p2_pe, G_base(u, p))
                                for ic in range(NC):
                                    tg = tr_base(u, p) + j * 32 + ic
                                    v.wait_ge(p2_tr, tg + 1)
                                    v.tensor_copy(
                                        hT[:, ic * 512 + j * 128:
                                           ic * 512 + j * 128 + 128],
                                        trp[tg % 4][:],
                                    ).then_inc(p2_trv, 1)
                            for oc in range(8):
                                for j in range(NT):
                                    v.wait_ge(p2_pe, G_base(u, p) + oc + 1)
                                    if p == 1:
                                        l = oc * 4 + j
                                        v.wait_ge(p2_gl, u * 32 + l + 1)
                                        src = gel[:, (l % 2) * 512:(l % 2) * 512 + 512]
                                    else:
                                        src = psb[j][:]
                                    v.scalar_tensor_tensor(
                                        x_sb[:, j * D + oc * 512:
                                             j * D + oc * 512 + 512],
                                        src, 1.0,
                                        x_sb[:, j * D + oc * 512:
                                             j * D + oc * 512 + 512],
                                        op0=ALU.mult, op1=ALU.add,
                                        accum_out=sumcols[:, j * 8 + oc:
                                                          j * 8 + oc + 1],
                                    ).then_inc(p2_ev, 1)
    return nc


def kernel(L0_attn_x, L0_attn_w1, L0_attn_w2, L0_attn_b2, L0_attn_sparse_mask,
           L0_ffn_qweight, L0_ffn_scales, L0_ffn_scaled_zeros, L0_ffn_bias):
    x = np.ascontiguousarray(np.asarray(L0_attn_x, dtype=np.float32).reshape(8192, D))
    w1 = np.ascontiguousarray(np.asarray(L0_attn_w1, dtype=np.float32))
    w2T = np.ascontiguousarray(np.asarray(L0_attn_w2, dtype=np.float32).T)
    b2 = np.ascontiguousarray(np.asarray(L0_attn_b2, dtype=np.float32).reshape(1, D))
    mask = np.ascontiguousarray(np.asarray(L0_attn_sparse_mask, dtype=np.float32))
    qw = np.ascontiguousarray(np.asarray(L0_ffn_qweight, dtype=np.int32).reshape(D, D))
    scal = np.ascontiguousarray(np.asarray(L0_ffn_scales, dtype=np.float32).reshape(D, 32))
    zer = np.ascontiguousarray(np.asarray(L0_ffn_scaled_zeros, dtype=np.float32).reshape(D, 32))
    fb = np.ascontiguousarray(np.asarray(L0_ffn_bias, dtype=np.float32).reshape(1, D))
    ident = np.eye(128, dtype=np.float32)

    nc = build_nc()
    in_maps = []
    for c in range(8):
        in_maps.append({
            "x_in": x[c * TCORE:(c + 1) * TCORE],
            "w1": w1, "w2T": w2T, "mask": mask, "b2": b2,
            "qw": qw, "scal": scal, "zer": zer, "fb": fb, "ident": ident,
        })
    res = run_bass_kernel_spmd(nc, in_maps, core_ids=list(range(8)))
    out = np.concatenate([res.results[c]["y_out"] for c in range(8)], axis=0)
    return out.reshape(1, 8192, D).astype(np.float32)


# version 25
# speedup vs baseline: 1.1417x; 1.0228x over previous
import numpy as np
import concourse.bass as bass
import concourse.mybir as mybir
from concourse.bass_utils import run_bass_kernel_spmd

F32 = mybir.dt.float32
BF16 = mybir.dt.bfloat16
I32 = mybir.dt.int32
AF = mybir.ActivationFunctionType
ALU = mybir.AluOpType
AX = mybir.AxisListType

D = 4096
TCORE = 1024
NT = 4
NC = 32
EPS = 1e-5

def cA(p):
    return 3 if p == 0 else 2

def cV(p):
    return 5 if p == 0 else 6

def A_base(u, p, j):
    return u * 28 + sum(4 * cA(q) for q in range(p)) + j * cA(p)

def V_base(u, p, j):
    return u * 68 + sum(4 * cV(q) for q in range(p)) + j * cV(p)

def tr_base(u, p):
    return (u * 3 + p) * 128

def G_base(u, p):
    return (u * 3 + p) * 8

def E_base(u, p):
    return (u * 3 + p) * 32

import math as _math

def _mask_support_tiles():
    n, window, stripes = D, 6, 3
    m = np.zeros((n, n), dtype=bool)
    stride = int(_math.sqrt(n))
    for k in range(stripes):
        ps = stride * k
        for i in range(0, n, window):
            m[ps + i:ps + i + window, i:i + window] = True
    m |= m.T
    return [[bool(m[S * 128:(S + 1) * 128, tc * 512:(tc + 1) * 512].any())
             for tc in range(8)] for S in range(32)]

KEEP = _mask_support_tiles()
K_TILES = []
P_PAIRS = []
PAIR_TILES = []
for _sb in range(4):
    for _tc in range(8):
        _sts = [st for st in range(8) if KEEP[_sb * 8 + st][_tc]]
        if _sts:
            P_PAIRS.append((_sb, _tc))
            PAIR_TILES.append(list(range(len(K_TILES), len(K_TILES) + len(_sts))))
            for st in _sts:
                K_TILES.append((_sb, _tc, st, _sb * 8 + st))
NK = len(K_TILES)
NP = len(P_PAIRS)
K_BEFORE = [PAIR_TILES[qq][0] for qq in range(NP)] + [NK]
SB_FIRST = {}
for qq, (sb, tc) in enumerate(P_PAIRS):
    SB_FIRST.setdefault(sb, qq)
TC_FIRST = []
_seen = set()
for (sb, tc) in P_PAIRS:
    TC_FIRST.append(tc not in _seen)
    _seen.add(tc)
TC_STORE_IDX = {}
_n = 0
for qq, (sb, tc) in enumerate(P_PAIRS):
    if TC_FIRST[qq]:
        TC_STORE_IDX[tc] = _n
        _n += 1
STORES_THRU = []
_n = 0
for qq in range(NP):
    if TC_FIRST[qq]:
        _n += 1
    STORES_THRU.append(_n)
WT_DONE = []
_tot = [0, 0]
for qq in range(NP):
    _tot[qq % 2] += 512 if TC_FIRST[qq] else 16
    WT_DONE.append(_tot[qq % 2])
WT_CNT = []
_fc = {}
for _qq in range(NP):
    _key = (TC_FIRST[_qq], _qq % 2)
    _fc[_key] = _fc.get(_key, 0) + 16
    WT_CNT.append(_fc[_key])

ATT_SC = [[S for S in range(32) if KEEP[S][oc]] for oc in range(8)]

QW_POS = [4 + (3 * r) // 2 for r in range(32)]
QW_AT = {}
DQ_AT = {}
TR_AT = {}
for _r, _p in enumerate(QW_POS):
    QW_AT.setdefault(_p, []).append(_r)
    DQ_AT.setdefault(_p + 2, []).append(_r)
    TR_AT.setdefault(_p + 4, []).append(_r)


def mv_slot_done(u, p, oc):
    g = (u * 3 + p) * 8 + oc
    return 16 * (g // 2 + 1)


def build_nc():
    nc = bass.Bass("TRN2", target_bir_lowering=False)

    x_in = nc.dram_tensor("x_in", [TCORE, D], F32, kind="ExternalInput").ap()
    w1 = nc.dram_tensor("w1", [D, D], F32, kind="ExternalInput").ap()
    w2T = nc.dram_tensor("w2T", [D, D], F32, kind="ExternalInput").ap()
    mask = nc.dram_tensor("mask", [D, D], F32, kind="ExternalInput").ap()
    b2 = nc.dram_tensor("b2", [1, D], F32, kind="ExternalInput").ap()
    qw = nc.dram_tensor("qw", [D, D], I32, kind="ExternalInput").ap()
    scal = nc.dram_tensor("scal", [D, 32], F32, kind="ExternalInput").ap()
    zer = nc.dram_tensor("zer", [D, 32], F32, kind="ExternalInput").ap()
    fb = nc.dram_tensor("fb", [1, D], F32, kind="ExternalInput").ap()
    ident_in = nc.dram_tensor("ident", [128, 128], F32, kind="ExternalInput").ap()
    y_out = nc.dram_tensor("y_out", [TCORE, D], F32, kind="ExternalOutput").ap()

    w2b = nc.dram_tensor("w2b", [D, D], BF16)
    amat = nc.dram_tensor("amat", [D, D], BF16)
    wdqT = nc.dram_tensor("wdqT", [D, D], BF16)

    from contextlib import ExitStack
    es = ExitStack()

    def sem1(name):
        return es.enter_context(nc.semaphore(name))

    def semn(name, n):
        return [es.enter_context(nc.semaphore(f"{name}{i}")) for i in range(n)]

    ms = sem1("ms")
    q_dq = sem1("q_dq")
    p1_pe = sem1("p1_pe")
    p1_ev = sem1("p1_ev")
    p2_tr = sem1("p2_tr")
    p2_trv = sem1("p2_trv")
    p2_pe = sem1("p2_pe")
    p2_ev = sem1("p2_ev")
    p2_gl = sem1("p2_gl")
    ac_ln = sem1("ac_ln")
    ve_ln = sem1("ve_ln")
    ms_ld = sem1("ms_ld")
    id_ld = sem1("id_ld")
    wqtr = sem1("wqtr")
    wqev = sem1("wqev")
    q_ld = semn("q_ld", 2)
    q_st = semn("q_st", 2)
    w1_lda = sem1("w1_lda")
    w1_ldb = sem1("w1_ldb")
    wb_st = semn("wb_st", 8)
    wt_ld = semn("wt_ld", 2)
    wr_ld = semn("wr_ld", 2)
    mk_ld = semn("mk_ld", 2)
    p1_st = semn("p1_st", 2)
    x_ld = sem1("x_ld")
    x_ld2 = sem1("x_ld2")
    st_o = semn("st_o", 4)
    mv_ld = semn("mv_ld", 2)

    with es:
        ident_bf = es.enter_context(nc.sbuf_tensor("ident_bf", [128, 128], BF16))
        ones_bf = es.enter_context(nc.sbuf_tensor("ones_bf", [1, 128], BF16))
        fb_bf = es.enter_context(nc.sbuf_tensor("fb_bf", [1, D], BF16))
        halves_bf = es.enter_context(nc.sbuf_tensor("halves_bf", [1, 128], BF16))
        b2_bf = es.enter_context(nc.sbuf_tensor("b2_bf", [1, D], BF16))
        stats = es.enter_context(nc.sbuf_tensor("stats", [128, 32], F32))
        sumcols = es.enter_context(nc.sbuf_tensor("sumcols", [128, 32], F32))
        with nc.Block() as block:
            from contextlib import ExitStack as _ES

            with _ES() as es_1:
                qt = es_1.enter_context(nc.sbuf_tensor("qt", [128, 2 * D], BF16))
                scl_sb = es_1.enter_context(nc.sbuf_tensor("scl_sb", [128, 1024], F32))
                zro_sb = es_1.enter_context(nc.sbuf_tensor("zro_sb", [128, 1024], F32))
                wout = es_1.enter_context(nc.sbuf_tensor("wout", [128, 2 * D], BF16))
                w1blk = es_1.enter_context(nc.sbuf_tensor("w1blk", [128, NC * 1024], BF16))
                w2t = es_1.enter_context(nc.sbuf_tensor("w2t", [128, 2 * NC * 512], BF16))
                mkt = es_1.enter_context(nc.sbuf_tensor("mkt", [128, 2 * 512], F32))
                asm2 = es_1.enter_context(nc.sbuf_tensor("asm2", [128, 2 * D], BF16))
                aev = es_1.enter_context(nc.sbuf_tensor("aev", [128, 2 * 512], BF16))
                pa = [es_1.enter_context(nc.psum_tensor(f"pa{i}", [128, 512], F32))
                      for i in range(2)]
                ptr = [es_1.enter_context(nc.psum_tensor(f"ptr{i}", [128, 128], BF16))
                       for i in range(4)]

                @block.gpsimd
                def _(g):
                    g.dma_start(out=ident_bf[:], in_=ident_in[:, :]).then_inc(id_ld, 16)
                    w1v = w1.rearrange("(c p) s -> p c s", p=128)
                    for h in range(4):
                        g.dma_start(
                            out=w1blk[:].rearrange("p (c s) -> p c s", s=1024)
                                [:, h * 8:(h + 1) * 8, 0:1024],
                            in_=w1v[:, h * 8:(h + 1) * 8, 0:1024],
                        ).then_inc(w1_lda if h < 2 else w1_ldb, 16)
                    g.dma_start(out=b2_bf[:], in_=b2[:, :]).then_inc(ms_ld, 16)
                    g.dma_start(out=fb_bf[:], in_=fb[:, :]).then_inc(ms_ld, 16)
                    g.dma_start(out=scl_sb[:].rearrange("p (r g) -> p r g", g=32),
                                in_=scal.rearrange("(r p) g -> p r g", p=128)
                                ).then_inc(ms_ld, 16)
                    g.dma_start(out=zro_sb[:].rearrange("p (r g) -> p r g", g=32),
                                in_=zer.rearrange("(r p) g -> p r g", p=128)
                                ).then_inc(ms_ld, 16)
                    def qw_load(r):
                        if r == 0:
                            g.wait_ge(ms, 4)
                        if r >= 2:
                            g.wait_ge(q_dq, 32 * (r - 1))
                        g.dma_start(out=qt[:, (r % 2) * D:(r % 2) * D + D],
                                    in_=qw[r * 128:(r + 1) * 128, :]
                                    ).then_inc(q_ld[r % 2], 16)
                    def emit_tiles(qq):
                        tc_ = P_PAIRS[qq][1]
                        for k in PAIR_TILES[qq]:
                            Sg = K_TILES[k][3]
                            g.wait_ge(p1_ev, k + 1)
                            g.dma_start(
                                out=amat[Sg * 128:(Sg + 1) * 128,
                                         tc_ * 512:(tc_ + 1) * 512],
                                in_=aev[:, (k % 2) * 512:(k % 2) * 512 + 512],
                            ).then_inc(p1_st[k % 2], 16)
                            for r in QW_AT.get(k, []):
                                qw_load(r)
                    for qq, (sb, tc) in enumerate(P_PAIRS):
                        if TC_FIRST[qq]:
                            if qq >= 2:
                                g.wait_ge(p1_pe, K_BEFORE[qq - 1])
                                if TC_FIRST[qq - 2]:
                                    g.wait_ge(wb_st[P_PAIRS[qq - 2][1]], 16)
                            sl2 = (qq % 2) * NC * 512
                            g.dma_start(
                                out=w2t[:, sl2:sl2 + NC * 512]
                                    .rearrange("p (c t) -> p c t", t=512),
                                in_=w2T.rearrange("(c p) t -> p c t", p=128)
                                    [:, :, tc * 512:(tc + 1) * 512],
                            ).then_inc(wt_ld[qq % 2], 16)
                        if qq >= 1:
                            emit_tiles(qq - 1)
                        if sb >= 1 and SB_FIRST[sb] == qq:
                            g.wait_ge(p1_pe, K_BEFORE[qq])
                            for h in range(4):
                                g.dma_start(
                                    out=w1blk[:]
                                        .rearrange("p (c s) -> p c s", s=1024)
                                        [:, h * 8:(h + 1) * 8, 0:1024],
                                    in_=w1v[:, h * 8:(h + 1) * 8,
                                            sb * 1024:(sb + 1) * 1024],
                                ).then_inc(w1_lda if h < 2 else w1_ldb, 16)
                    emit_tiles(NP - 1)

                @block.sync
                def _(sync):
                    def emit_masks(qq):
                        tc_ = P_PAIRS[qq][1]
                        for k in PAIR_TILES[qq]:
                            Sg = K_TILES[k][3]
                            if k >= 2:
                                sync.wait_ge(p1_ev, k - 1)
                            sync.dma_start(
                                out=mkt[:, (k % 2) * 512:(k % 2) * 512 + 512],
                                in_=mask[Sg * 128:(Sg + 1) * 128,
                                         tc_ * 512:(tc_ + 1) * 512],
                            ).then_inc(mk_ld[k % 2], 16)
                    for qq, (sb, tc) in enumerate(P_PAIRS):
                        if not TC_FIRST[qq]:
                            sync.wait_ge(p1_pe, K_BEFORE[qq - 1])
                            if TC_FIRST[qq - 2]:
                                sync.wait_ge(wb_st[P_PAIRS[qq - 2][1]], 16)
                            sync.wait_ge(wb_st[tc], 16)
                            sl2 = (qq % 2) * NC * 512
                            sync.dma_start(
                                out=w2t[:, sl2:sl2 + NC * 512]
                                    .rearrange("p (c t) -> p c t", t=512),
                                in_=w2b.rearrange("(c p) t -> p c t", p=128)
                                    [:, :, tc * 512:(tc + 1) * 512],
                            ).then_inc(wr_ld[qq % 2], 16)
                        if qq >= 1:
                            emit_masks(qq - 1)
                    emit_masks(NP - 1)
                @block.vector
                def _(v):
                    v.memset(ones_bf[:], 1.0).then_inc(ms, 1)
                    v.memset(halves_bf[:], 0.5).then_inc(ms, 1)
                    v.wait_ge(ms_ld, 64)
                    v.tensor_scalar_mul(scl_sb[:], scl_sb[:], 0.5).then_inc(ms, 1)
                    v.wait_ge(ms, 3)
                    v.tensor_scalar_mul(zro_sb[:], zro_sb[:], 0.5).then_inc(ms, 1)
                    def dequant_tile(r):
                        v.wait_ge(q_ld[r % 2], 16 * (r // 2 + 1))
                        if r >= 2:
                            v.wait_ge(wqtr, 32 * (r - 1))
                        for gg in range(32):
                            sl = r % 2
                            v.tensor_scalar(
                                wout[:, sl * D + gg * 128: sl * D + gg * 128 + 128],
                                qt[:, sl * D + gg * 128: sl * D + gg * 128 + 128],
                                scl_sb[:, r * 32 + gg: r * 32 + gg + 1],
                                zro_sb[:, r * 32 + gg: r * 32 + gg + 1],
                                op0=ALU.mult, op1=ALU.add,
                            ).then_inc(q_dq, 1)
                    for k in range(NK):
                        v.wait_ge(p1_pe, k + 1)
                        v.wait_ge(mk_ld[k % 2], 16 * (k // 2 + 1))
                        if k >= 2:
                            v.wait_ge(p1_st[k % 2], 16 * (k // 2))
                        v.tensor_tensor(
                            aev[:, (k % 2) * 512:(k % 2) * 512 + 512],
                            pa[k % 2][:],
                            mkt[:, (k % 2) * 512:(k % 2) * 512 + 512],
                            op=ALU.mult,
                        ).then_inc(p1_ev, 1)
                        for r in TR_AT.get(k, []):
                            if r >= 2:
                                v.wait_ge(q_st[r % 2], 16 * (r // 2))
                            for blk in range(NC):
                                b = r * NC + blk
                                v.wait_ge(wqtr, b + 1)
                                v.tensor_copy(
                                    asm2[:, (r % 2) * D + blk * 128:
                                         (r % 2) * D + blk * 128 + 128],
                                    ptr[b % 4][:],
                                ).then_inc(wqev, 1)
                        for r in DQ_AT.get(k, []):
                            dequant_tile(r)

                @block.scalar
                def _(s):
                    acts = []
                    for qq, (sb, tc) in enumerate(P_PAIRS):
                        if TC_FIRST[qq]:
                            acts.append((K_BEFORE[qq], 0, qq))
                    for r in range(32):
                        acts.append((QW_POS[r] + 4, 1, r))
                    for _, kind, idx in sorted(acts):
                        if kind == 0:
                            qq = idx
                            tc = P_PAIRS[qq][1]
                            sl2 = (qq % 2) * NC * 512
                            s.wait_ge(wt_ld[qq % 2], WT_CNT[qq])
                            s.dma_start(
                                out=w2b.rearrange("(c p) t -> p c t", p=128)
                                    [:, :, tc * 512:(tc + 1) * 512],
                                in_=w2t[:, sl2:sl2 + NC * 512]
                                    .rearrange("p (c t) -> p c t", t=512),
                            ).then_inc(wb_st[tc], 16)
                        else:
                            r = idx
                            s.wait_ge(wqev, 32 * (r + 1))
                            s.dma_start(out=wdqT.rearrange("(c p) o -> p c o", p=128)
                                        [:, :, r * 128:(r + 1) * 128],
                                        in_=asm2[:, (r % 2) * D:(r % 2) * D + D]
                                        .rearrange("p (c o) -> p c o", o=128)
                                        ).then_inc(q_st[r % 2], 16)

                @block.tensor
                def _(t):
                    t.wait_ge(id_ld, 16)
                    def wq_transpose_tile(r):
                        t.wait_ge(q_dq, 32 * (r + 1))
                        for blk in range(NC):
                            b = r * NC + blk
                            if b >= 4:
                                t.wait_ge(wqev, b - 3)
                            t.transpose(ptr[b % 4][:],
                                        wout[:, (r % 2) * D + blk * 128:
                                             (r % 2) * D + blk * 128 + 128],
                                        ident_bf[:]).then_inc(wqtr, 1)
                    seen_sb = set()
                    for qq, (sb, tc) in enumerate(P_PAIRS):
                        new_sb = sb not in seen_sb
                        if new_sb:
                            seen_sb.add(sb)
                            t.wait_ge(w1_lda, 32 * (sb + 1))
                        if TC_FIRST[qq]:
                            t.wait_ge(wt_ld[qq % 2], WT_CNT[qq])
                        else:
                            t.wait_ge(wr_ld[qq % 2], WT_CNT[qq])
                        for k in PAIR_TILES[qq]:
                            st = K_TILES[k][2]
                            if k >= 2:
                                t.wait_ge(p1_ev, k - 1)
                            for dc in range(NC):
                                if new_sb and k == PAIR_TILES[qq][0] and dc == 16:
                                    t.wait_ge(w1_ldb, 32 * (sb + 1))
                                ins = t.matmul(
                                    pa[k % 2][:],
                                    w1blk[:, dc * 1024 + st * 128:
                                          dc * 1024 + st * 128 + 128],
                                    w2t[:, (qq % 2) * NC * 512 + dc * 512:
                                        (qq % 2) * NC * 512 + dc * 512 + 512],
                                    start=(dc == 0), stop=(dc == NC - 1),
                                )
                            ins.then_inc(p1_pe, 1)
                            for r in TR_AT.get(k, []):
                                wq_transpose_tile(r)

            with _ES() as es_2:
                x_sb = es_2.enter_context(nc.sbuf_tensor("x_sb", [128, NT * D], F32))
                hT = es_2.enter_context(nc.sbuf_tensor("hT", [128, NC * 512], BF16))
                mv = es_2.enter_context(nc.sbuf_tensor("mv", [128, 2 * NC * 512], BF16))
                h_tmp = es_2.enter_context(nc.sbuf_tensor("h_tmp", [128, D], BF16))
                sq_scr = es_2.enter_context(nc.sbuf_tensor("sq_scr", [128, D], BF16))
                gel = es_2.enter_context(nc.sbuf_tensor("gel", [128, 2 * 512], F32))
                psb = [es_2.enter_context(nc.psum_tensor(f"ps{i}", [128, 512], F32))
                       for i in range(4)]
                trp = [es_2.enter_context(nc.psum_tensor(f"tr{i}", [128, 128], BF16))
                       for i in range(4)]

                @block.gpsimd
                def _(g):
                    for j in range(NT):
                        g.wait_ge(st_o[j], 16)
                        g.dma_start(
                            out=x_sb[:, j * D:(j + 1) * D],
                            in_=x_in[512 + j * 128:512 + (j + 1) * 128, :],
                        ).then_inc(x_ld2, 16)

                @block.sync
                def _(sync):
                    sync.wait_ge(p1_pe, NK)
                    sync.wait_ge(p1_ev, NK)
                    for _tc in range(8):
                        sync.wait_ge(wb_st[_tc], 16)
                    sync.wait_ge(p1_st[0], 16 * ((NK + 1) // 2))
                    sync.wait_ge(p1_st[1], 16 * (NK // 2))
                    sync.wait_ge(q_dq, 1024)
                    sync.wait_ge(q_st[0], 256)
                    sync.wait_ge(q_st[1], 256)

                    for u in range(2):
                        if u == 0:
                            for j in range(NT):
                                sync.dma_start(
                                    out=x_sb[:, j * D:(j + 1) * D],
                                    in_=x_in[j * 128:(j + 1) * 128, :],
                                ).then_inc(x_ld, 16)
                        for p in range(3):
                            for oc in range(8):
                                g = G_base(u, p) + oc
                                if g >= 2:
                                    sync.wait_ge(p2_pe, g - 1)
                                sl = (g % 2) * NC * 512
                                if p == 1:
                                    S0, nk = ATT_SC[oc][0], len(ATT_SC[oc])
                                    sync.dma_start(
                                        out=mv[:, sl:sl + nk * 512]
                                            .rearrange("p (c t) -> p c t", t=512),
                                        in_=amat.rearrange("(c p) t -> p c t",
                                                           p=128)
                                            [:, S0:S0 + nk,
                                             oc * 512:(oc + 1) * 512],
                                    ).then_inc(mv_ld[g % 2], 16)
                                else:
                                    if (u, p, oc) == (0, 0, 0):
                                        sync.wait_ge(q_st[0], 256)
                                        sync.wait_ge(q_st[1], 256)
                                    sync.dma_start(
                                        out=mv[:, sl:sl + NC * 512]
                                            .rearrange("p (c t) -> p c t", t=512),
                                        in_=wdqT.rearrange("(c p) o -> p c o",
                                                           p=128)
                                            [:, :, oc * 512:(oc + 1) * 512],
                                    ).then_inc(mv_ld[g % 2], 16)
                        for j in range(NT):
                            sync.wait_ge(p2_ev, E_base(u, 2) + 29 + j)
                            sync.dma_start(
                                out=y_out[u * 512 + j * 128:
                                          u * 512 + (j + 1) * 128, :],
                                in_=x_sb[:, j * D:(j + 1) * D],
                            ).then_inc(st_o[j], 16)
                    for j in range(NT):
                        sync.wait_ge(st_o[j], 32)

                @block.scalar
                def _(s):
                    for _tc in range(8):
                        s.wait_ge(wb_st[_tc], 16)
                    for u in range(2):
                        for p in range(3):
                            for j in range(NT):
                                a0 = A_base(u, p, j)
                                v0 = V_base(u, p, j)
                                s.wait_ge(ac_ln, a0)
                                if p == 0:
                                    s.wait_ge(x_ld if u == 0 else x_ld2, 64)
                                    if u == 1:
                                        s.wait_ge(p2_ev, 96)
                                    s.activation(sq_scr[:], x_sb[:, j * D:(j + 1) * D],
                                                 AF.Copy,
                                                 accum_out=stats[:, j:j + 1]
                                                 ).then_inc(ac_ln, 1)
                                    s.wait_ge(ac_ln, a0 + 1)
                                else:
                                    s.wait_ge(p2_ev, E_base(u, p - 1) + 29 + j)
                                s.activation(sq_scr[:], x_sb[:, j * D:(j + 1) * D],
                                             AF.Square,
                                             accum_out=stats[:, 4 + j:5 + j]
                                             ).then_inc(ac_ln, 1)
                                tvar_idx = 3 if p == 0 else 4
                                s.wait_ge(ve_ln, v0 + tvar_idx)
                                s.activation(stats[:, 16 + j:17 + j],
                                             stats[:, 4 + j:5 + j],
                                             AF.Sqrt,
                                             bias=stats[:, 12 + j:13 + j],
                                             scale=1.0 / D).then_inc(ac_ln, 1)
                            if p == 1:
                                for oc in range(8):
                                    for j in range(NT):
                                        l = oc * 4 + j
                                        s.wait_ge(p2_pe, G_base(u, 1) + oc + 1)
                                        if l >= 2:
                                            s.wait_ge(p2_ev, E_base(u, 1) + l - 1)
                                        s.activation(
                                            gel[:, (l % 2) * 512:(l % 2) * 512 + 512],
                                            psb[j][:],
                                            AF.Gelu).then_inc(p2_gl, 1)

                @block.tensor
                def _(t):
                    t.wait_ge(p1_ev, NK)
                    t.wait_ge(ms_ld, 64)
                    t.wait_ge(ms, 2)
                    for u in range(2):
                        for p in range(3):
                            for j in range(NT):
                                t.wait_ge(ve_ln, V_base(u, p, j) + cV(p))
                                for ic in range(NC):
                                    tg = tr_base(u, p) + j * 32 + ic
                                    if tg >= 4:
                                        t.wait_ge(p2_trv, tg - 3)
                                    t.transpose(trp[tg % 4][:],
                                                h_tmp[:, ic * 128:(ic + 1) * 128],
                                                ident_bf[:]).then_inc(p2_tr, 1)
                            for oc in range(8):
                                g = G_base(u, p) + oc
                                if oc == 0:
                                    t.wait_ge(p2_trv, tr_base(u, p) + 128)
                                t.wait_ge(mv_ld[g % 2], mv_slot_done(u, p, oc))
                                t.wait_ge(p2_ev, E_base(u, p) + 4 * oc)
                                if p == 1:
                                    t.wait_ge(p2_gl, u * 32 + 4 * oc)
                                if p == 2:
                                    t.wait_ge(p2_gl, u * 32 + 32)
                                sl = (g % 2) * NC * 512
                                if p == 1:
                                    chunks = ATT_SC[oc]
                                else:
                                    chunks = list(range(NC))
                                for ii, ic in enumerate(chunks):
                                    for j in range(NT):
                                        t.matmul(
                                            psb[j][:],
                                            hT[:, ic * 512 + j * 128:
                                               ic * 512 + j * 128 + 128],
                                            mv[:, sl + ii * 512:sl + ii * 512 + 512],
                                            start=(ii == 0), stop=False,
                                        )
                                brow = b2_bf if p == 1 else fb_bf
                                lrow = ones_bf if p == 1 else halves_bf
                                for j in range(NT):
                                    ins = t.matmul(
                                        psb[j][:],
                                        lrow[0:1, :],
                                        brow[0:1, oc * 512:(oc + 1) * 512],
                                        start=False, stop=True,
                                    )
                                ins.then_inc(p2_pe, 1)

                @block.vector
                def _(v):
                    for _tc in range(8):
                        v.wait_ge(wb_st[_tc], 16)
                    for u in range(2):
                        for p in range(3):
                            for j in range(NT):
                                a0 = A_base(u, p, j)
                                v0 = V_base(u, p, j)
                                if p > 0:
                                    v.wait_ge(p2_ev, E_base(u, p - 1) + 29 + j)
                                    v.tensor_reduce(stats[:, j:j + 1],
                                                    sumcols[:, j * 8:j * 8 + 8],
                                                    axis=AX.X, op=ALU.add
                                                    ).then_inc(ve_ln, 1)
                                m_idx = 1 if p == 0 else 2
                                v.wait_ge(ac_ln, a0 + 2 if p == 0 else a0 + 1)
                                if p != 0:
                                    v.wait_ge(ve_ln, v0 + 1)
                                v.tensor_scalar_mul(stats[:, 8 + j:9 + j],
                                                    stats[:, j:j + 1], 1.0 / D
                                                    ).then_inc(ve_ln, 1)
                                v.wait_ge(ve_ln, v0 + m_idx)
                                v.tensor_mul(stats[:, 12 + j:13 + j],
                                             stats[:, 8 + j:9 + j],
                                             stats[:, 8 + j:9 + j]
                                             ).then_inc(ve_ln, 1)
                                v.wait_ge(ve_ln, v0 + m_idx + 1)
                                v.tensor_scalar(stats[:, 12 + j:13 + j],
                                                stats[:, 12 + j:13 + j],
                                                -1.0, EPS,
                                                op0=ALU.mult, op1=ALU.add
                                                ).then_inc(ve_ln, 1)
                                v.wait_ge(ac_ln, a0 + cA(p))
                                v.reciprocal(stats[:, 20 + j:21 + j],
                                             stats[:, 16 + j:17 + j]
                                             ).then_inc(ve_ln, 1)
                                v.wait_ge(ve_ln, v0 + m_idx + 3)
                                v.wait_ge(p2_tr, tr_base(u, p) + j * 32)
                                v.tensor_scalar(h_tmp[:],
                                                x_sb[:, j * D:(j + 1) * D],
                                                stats[:, 8 + j:9 + j],
                                                stats[:, 20 + j:21 + j],
                                                op0=ALU.subtract, op1=ALU.mult
                                                ).then_inc(ve_ln, 1)
                                if j == 0:
                                    v.wait_ge(p2_pe, G_base(u, p))
                                for ic in range(NC):
                                    tg = tr_base(u, p) + j * 32 + ic
                                    v.wait_ge(p2_tr, tg + 1)
                                    v.tensor_copy(
                                        hT[:, ic * 512 + j * 128:
                                           ic * 512 + j * 128 + 128],
                                        trp[tg % 4][:],
                                    ).then_inc(p2_trv, 1)
                            for oc in range(8):
                                for j in range(NT):
                                    v.wait_ge(p2_pe, G_base(u, p) + oc + 1)
                                    if p == 1:
                                        l = oc * 4 + j
                                        v.wait_ge(p2_gl, u * 32 + l + 1)
                                        src = gel[:, (l % 2) * 512:(l % 2) * 512 + 512]
                                    else:
                                        src = psb[j][:]
                                    v.scalar_tensor_tensor(
                                        x_sb[:, j * D + oc * 512:
                                             j * D + oc * 512 + 512],
                                        src, 1.0,
                                        x_sb[:, j * D + oc * 512:
                                             j * D + oc * 512 + 512],
                                        op0=ALU.mult, op1=ALU.add,
                                        accum_out=sumcols[:, j * 8 + oc:
                                                          j * 8 + oc + 1],
                                    ).then_inc(p2_ev, 1)
    return nc


def kernel(L0_attn_x, L0_attn_w1, L0_attn_w2, L0_attn_b2, L0_attn_sparse_mask,
           L0_ffn_qweight, L0_ffn_scales, L0_ffn_scaled_zeros, L0_ffn_bias):
    x = np.ascontiguousarray(np.asarray(L0_attn_x, dtype=np.float32).reshape(8192, D))
    w1 = np.ascontiguousarray(np.asarray(L0_attn_w1, dtype=np.float32))
    w2T = np.ascontiguousarray(np.asarray(L0_attn_w2, dtype=np.float32).T)
    b2 = np.ascontiguousarray(np.asarray(L0_attn_b2, dtype=np.float32).reshape(1, D))
    mask = np.ascontiguousarray(np.asarray(L0_attn_sparse_mask, dtype=np.float32))
    qw = np.ascontiguousarray(np.asarray(L0_ffn_qweight, dtype=np.int32).reshape(D, D))
    scal = np.ascontiguousarray(np.asarray(L0_ffn_scales, dtype=np.float32).reshape(D, 32))
    zer = np.ascontiguousarray(np.asarray(L0_ffn_scaled_zeros, dtype=np.float32).reshape(D, 32))
    fb = np.ascontiguousarray(np.asarray(L0_ffn_bias, dtype=np.float32).reshape(1, D))
    ident = np.eye(128, dtype=np.float32)

    nc = build_nc()
    in_maps = []
    for c in range(8):
        in_maps.append({
            "x_in": x[c * TCORE:(c + 1) * TCORE],
            "w1": w1, "w2T": w2T, "mask": mask, "b2": b2,
            "qw": qw, "scal": scal, "zer": zer, "fb": fb, "ident": ident,
        })
    res = run_bass_kernel_spmd(nc, in_maps, core_ids=list(range(8)))
    out = np.concatenate([res.results[c]["y_out"] for c in range(8)], axis=0)
    return out.reshape(1, 8192, D).astype(np.float32)
